# version 02
# baseline (speedup 1.0000x reference)
from contextlib import ExitStack

import numpy as np
import ml_dtypes
import concourse.bass as bass
import concourse.mybir as mybir
from concourse.bass_utils import run_bass_kernel_spmd

F32 = mybir.dt.float32
BF16 = mybir.dt.bfloat16
I32 = mybir.dt.int32
AF = mybir.ActivationFunctionType
OP = mybir.AluOpType
AX = mybir.AxisListType

NCORES = 8
B, S, D = 32, 2048, 1024
NSEQ = B // NCORES
NTOK = NSEQ * S
NCH = NTOK // 128
H, DH = 16, 64
E = 8
CAP = 10240.0
EPS = 1e-5
NEG = -1.0e30
NC_SEQ = S // 128
SIM_CC_COPY = False
DEBUG_X = False
NQG = S // 512


class Em:

    ALL = ("t", "v", "s", "g", "q")
    NWAVE = 6

    def __init__(self, nc):
        self.nc = nc
        self.es = ExitStack()
        self.sem = {}
        for name in ("t", "v", "s", "g"):
            self.sem[name] = self.es.enter_context(nc.semaphore("sem_" + name))
        for qe in ("q", "g", "s"):
            for i in range(Em.NWAVE):
                name = f"d{qe}{i}"
                self.sem[name] = self.es.enter_context(nc.semaphore("sem_" + name))
        self.cnt = {k: 0 for k in self.sem}
        self.ops = {e: [] for e in Em.ALL}
        self.lw = {}
        self.rd = {}
        self.wm = {e: {sn: 0 for sn in self.sem} for e in Em.ALL}
        self.consts = None
        self.wrot = {qe: 0 for qe in ("q", "g", "s")}
        self.wcur = {qe: None for qe in ("q", "g", "s")}
        self.waves = {}
        self._wid = 0

    def _seal(self, qe):
        w = self.wcur[qe]
        if w is None:
            return
        self.waves[w["id"]] = self.cnt[w["sem"]]
        self.wcur[qe] = None

    def seal_all(self):
        for qe in ("q", "g", "s"):
            self._seal(qe)

    def _resolve(self, dep):
        sn, c = dep
        if isinstance(c, tuple) and c[0] == "wave":
            wid, qe = c[1], c[2]
            if wid not in self.waves:
                self._seal(qe)
            return (sn, self.waves[wid])
        return (sn, c)

    def emit(self, e, fn, reads=(), writes=(), dma=False):
        deps = {}

        def add(sn, c):
            if c > deps.get(sn, 0):
                deps[sn] = c

        for k in reads:
            if k in self.lw:
                add(*self._resolve(self.lw[k]))
        for k in writes:
            if k in self.lw:
                add(*self._resolve(self.lw[k]))
            for s2, c2 in self.rd.get(k, {}).items():
                add(*self._resolve((s2, c2)))

        pre_gate = None
        if dma:
            qe = e
            w = self.wcur[qe]
            if w is None:
                idx = self.wrot[qe]
                self.wrot[qe] = (idx + 1) % Em.NWAVE
                semname = f"d{qe}{idx}"
                self._wid += 1
                w = dict(id=self._wid, sem=semname)
                self.wcur[qe] = w
                if self.cnt[semname] > 0:
                    pre_gate = (semname, self.cnt[semname])
                    self.wm[e][semname] = max(self.wm[e][semname],
                                              self.cnt[semname])
            semname = w["sem"]
            marker = ("wave", w["id"], qe)
        else:
            semname = e
            marker = None

        waits = []
        for s2, c2 in sorted(deps.items(), key=str):
            if e == "t" and s2 == "t":
                continue
            if self.wm[e][s2] >= c2:
                continue
            assert c2 <= self.cnt[s2], f"future wait {s2}={c2}>{self.cnt[s2]}"
            waits.append((s2, c2))
            self.wm[e][s2] = c2
        if pre_gate is not None:
            waits.insert(0, pre_gate)
        inc = 16 if dma else 1
        self.cnt[semname] += inc
        mycnt = self.cnt[semname]
        sems = self.sem

        def run(engobj, waits=waits, fn=fn, sname=semname, inc=inc):
            for s2, c2 in waits:
                engobj.wait_ge(sems[s2], c2)
            fn(engobj).then_inc(sems[sname], inc)

        self.ops[e].append(run)
        lwval = (semname, marker if dma else mycnt)
        for k in writes:
            self.lw[k] = lwval
            self.rd[k] = {}
        for k in reads:
            d = self.rd.setdefault(k, {})
            d[semname] = marker if dma else mycnt
        return lwval

    def barrier(self, engines=ALL):
        self.seal_all()
        snap = dict(self.cnt)
        sems = self.sem
        for e in engines:
            waits = []
            for sn, c in sorted(snap.items()):
                if c > 0 and self.wm[e][sn] < c:
                    waits.append((sn, c))
                    self.wm[e][sn] = c
            if not waits:
                continue

            def run(engobj, waits=waits):
                for sn, c in waits:
                    engobj.wait_ge(sems[sn], c)

            self.ops[e].append(run)

    def finalize(self):
        nc = self.nc
        self.barrier(engines=("q",))
        with nc.Block() as block:
            for ename, dec in (("t", block.tensor), ("v", block.vector),
                               ("s", block.scalar), ("g", block.gpsimd),
                               ("q", block.sync)):
                ops = self.ops[ename]

                def body(engobj, ops=ops):
                    for op in ops:
                        op(engobj)

                dec(body)
        self.es.close()


_UID = [0]


def _uname(name):
    _UID[0] += 1
    return f"{name}_u{_UID[0]}"


def sb(em, stack, name, shape, dtype):
    return stack.enter_context(em.nc.sbuf_tensor(_uname(name), shape, dtype))


def ps(em, stack, name, shape, dtype):
    return stack.enter_context(em.nc.psum_tensor(_uname(name), shape, dtype))


def emit_consts(em, stack):
    ii = sb(em, stack, "c_ii", [128, 128], I32)
    ident = sb(em, stack, "c_ident", [128, 128], BF16)
    trim = sb(em, stack, "c_trim", [128, 128], F32)
    tris = sb(em, stack, "c_tris", [128, 128], F32)
    em.emit("g", lambda g: g.iota(ii[:, :], pattern=[[1, 128]], base=0,
                                  channel_multiplier=-1), writes=["c_ii"])
    em.emit("v", lambda v: v.tensor_scalar(ident[:, :], ii[:, :], 0, None,
                                           op0=OP.is_equal),
            reads=["c_ii"], writes=["c_ident"])
    em.emit("v", lambda v: v.tensor_scalar(trim[:, :], ii[:, :], 0, NEG,
                                           op0=OP.is_gt, op1=OP.mult),
            reads=["c_ii"], writes=["c_trim"])
    em.emit("v", lambda v: v.tensor_scalar(tris[:, :], ii[:, :], 0, None,
                                           op0=OP.is_gt),
            reads=["c_ii"], writes=["c_tris"])
    eps = sb(em, stack, "c_eps", [128, 1], F32)
    em.emit("v", lambda v: v.memset(eps[:, :], EPS), writes=["c_eps"])
    em.consts = dict(ident=ident, trim=trim, tris=tris, eps=eps)


def emit_cast(em, dst_ap, src_ap, key):
    em.emit("g", lambda g: g.dma_start(out=dst_ap, in_=src_ap,
                                       max_dma_last_dim=8192),
            writes=[key], dma=True)


def emit_ln_chunk(em, lnb, x_sb, xkey, xn_ap, xnkey):
    bnst, mv, sdv, rstd = lnb["bnst"], lnb["mv"], lnb["sdv"], lnb["rstd"]
    for i in range(2):
        em.emit("v", lambda v, i=i: v.bn_stats(bnst[:, 6 * i:6 * i + 6],
                                               x_sb[:, 512 * i:512 * (i + 1)]),
                reads=[xkey], writes=["ln_bnst"])
    em.emit("v", lambda v: v.bn_aggr(mv[:, :], bnst[:, :]),
            reads=["ln_bnst"], writes=["ln_mv"])
    epsv = em.consts["eps"]
    em.emit("s", lambda s_: s_.activation(sdv[:, :], mv[:, 1:2], AF.Sqrt,
                                          bias=epsv[:, :]),
            reads=["ln_mv", "c_eps"], writes=["ln_sdv"])
    em.emit("v", lambda v: v.reciprocal(rstd[:, :], sdv[:, :]),
            reads=["ln_sdv"], writes=["ln_rstd"])
    em.emit("v", lambda v: v.tensor_scalar(xn_ap, x_sb[:, :], mv[:, 0:1],
                                           rstd[:, :], op0=OP.subtract,
                                           op1=OP.mult),
            reads=[xkey, "ln_mv", "ln_rstd"], writes=[xnkey])


def mk_lnb(em, stack, tag):
    return dict(
        bnst=sb(em, stack, f"{tag}_bnst", [128, 12], F32),
        mv=sb(em, stack, f"{tag}_mv", [128, 2], F32),
        sdv=sb(em, stack, f"{tag}_sdv", [128, 1], F32),
        rstd=sb(em, stack, f"{tag}_rstd", [128, 1], F32),
    )


def emit_attn_phase(em, stack, X_in, X_out, Wq, Wk, Wv, Wo, rope_cos, rope_sin,
                    layer):
    nc = em.nc
    L = f"L{layer}a"
    ident, trim = em.consts["ident"], em.consts["trim"]

    wq_b = nc.dram_tensor(f"{L}_wq_b", [D, D], BF16).ap()
    wk_b = nc.dram_tensor(f"{L}_wk_b", [D, D], BF16).ap()
    wv_b = nc.dram_tensor(f"{L}_wv_b", [D, D], BF16).ap()
    wo_b = nc.dram_tensor(f"{L}_wo_b", [D, D], BF16).ap()
    emit_cast(em, wq_b, Wq, f"{L}wqb")
    emit_cast(em, wk_b, Wk, f"{L}wkb")
    emit_cast(em, wv_b, Wv, f"{L}wvb")
    emit_cast(em, wo_b, Wo, f"{L}wob")

    ph = ExitStack()
    qfm = sb(em, ph, f"{L}_qfm", [128, 8, S], BF16)
    kfm = sb(em, ph, f"{L}_kfm", [128, 8, S], BF16)
    vtm = sb(em, ph, f"{L}_vtm", [128, NC_SEQ, D], BF16)
    lnb = mk_lnb(em, ph, L)

    def hv(x, off, lo):
        r = x[:, off:off + 1024].rearrange("p (h x) -> p h x", h=16)
        return r[:, :, 0:32] if lo else r[:, :, 32:64]

    def one_seq(s):
        row0 = s * S
        ab = ExitStack()
        wqt = sb(em, ab, f"{L}_wqt", [128, 8, D], BF16)
        wkt = sb(em, ab, f"{L}_wkt", [128, 8, D], BF16)
        wvt = sb(em, ab, f"{L}_wvt", [128, 8, D], BF16)
        xch = [sb(em, ab, f"{L}_xch{i}", [128, D], F32) for i in range(2)]
        xntm = [sb(em, ab, f"{L}_xntm{i}", [128, D], BF16) for i in range(2)]
        xnfm = [sb(em, ab, f"{L}_xnfm{i}", [128, 8, 128], BF16) for i in range(2)]
        qktm = [sb(em, ab, f"{L}_qktm{i}", [128, 2048], BF16) for i in range(2)]
        cstile = [sb(em, ab, f"{L}_cs{i}", [128, D], BF16) for i in range(2)]
        sntile = [sb(em, ab, f"{L}_sn{i}", [128, D], BF16) for i in range(2)]
        rtmp = sb(em, ab, f"{L}_rtmp", [128, D], BF16)
        pp = [ps(em, ab, f"{L}_pp{i}", [128, 512], F32) for i in range(2)]
        ptr = [ps(em, ab, f"{L}_ptr{i}", [128, 1024], BF16) for i in range(2)]

        for wt, wb_, wkey in ((wqt, wq_b, f"{L}wqb"), (wkt, wk_b, f"{L}wkb"),
                              (wvt, wv_b, f"{L}wvb")):
            for j in range(8):
                em.emit("q", lambda q, wt=wt, wb_=wb_, j=j: q.dma_start(
                    out=wt[:, j, :], in_=wb_[0:D, j * 128:(j + 1) * 128],
                    transpose=True),
                    reads=[wkey], writes=[(wt.name, j)], dma=True)

        for c in range(NC_SEQ):
            b = c % 2
            xk = ("xch", b)
            em.emit("q", lambda q, b=b, c=c: q.dma_start(
                out=xch[b][:, :],
                in_=X_in[row0 + c * 128: row0 + (c + 1) * 128, :]),
                writes=[xk], dma=True)
            if layer == 0:
                em.emit("q", lambda q, b=b, c=c: q.dma_start(
                    out=cstile[b][:, :], in_=rope_cos[c * 128:(c + 1) * 128, :]),
                    writes=[("cs", b)], dma=True)
                em.emit("q", lambda q, b=b, c=c: q.dma_start(
                    out=sntile[b][:, :], in_=rope_sin[c * 128:(c + 1) * 128, :]),
                    writes=[("sn", b)], dma=True)
            emit_ln_chunk(em, lnb, xch[b], xk, xntm[b][:, :], ("xntm", b))
            for j in range(8):
                em.emit("t", lambda t, b=b, j=j: t.transpose(
                    ptr[0][:, j * 128:(j + 1) * 128],
                    xntm[b][:, j * 128:(j + 1) * 128], ident[:, :]),
                    reads=[("xntm", b), "c_ident"], writes=[("ptr", 0)])
            em.emit("v", lambda v, b=b: v.tensor_copy(xnfm[b][:, :, :],
                                                      ptr[0][:, :]),
                    reads=[("ptr", 0)], writes=[("xnfm", b)])
            for pi, (isv, off) in enumerate(((False, 0), (False, 1024),
                                             (True, 0))):
                wt = (wqt, wkt, wvt)[pi]
                for half in range(2):
                    pb = pp[half]
                    for j in range(8):
                        em.emit("t", lambda t, pb=pb, b=b, j=j, wt=wt, half=half:
                                t.matmul(pb[:, :], xnfm[b][:, j, :],
                                         wt[:, j, half * 512:(half + 1) * 512],
                                         start=(j == 0), stop=(j == 7)),
                                reads=[("xnfm", b), (wt.name, j)],
                                writes=[("pp", half)])
                    if isv:
                        out_ap = vtm[:, c, half * 512:(half + 1) * 512]
                        out_key = ("vtm", c)
                    else:
                        out_ap = qktm[b][:, off + half * 512:off + (half + 1) * 512]
                        out_key = ("qktm", b)
                    em.emit("v", lambda v, pb=pb, out_ap=out_ap: v.tensor_copy(
                        out_ap, pb[:, :]),
                        reads=[("pp", half)], writes=[out_key])
            if layer == 0:
                for off in (0, 1024):
                    qk = ("qktm", b)
                    em.emit("v", lambda v, b=b, off=off: v.tensor_tensor(
                        hv(rtmp, 0, True), hv(qktm[b], off, False),
                        hv(sntile[b], 0, True), op=OP.mult),
                        reads=[qk, ("sn", b)], writes=[("rtmp", 0)])
                    em.emit("v", lambda v, b=b, off=off: v.tensor_tensor(
                        hv(rtmp, 0, False), hv(qktm[b], off, True),
                        hv(sntile[b], 0, False), op=OP.mult),
                        reads=[qk, ("sn", b)], writes=[("rtmp", 1)])
                    em.emit("v", lambda v, b=b, off=off: v.tensor_tensor(
                        hv(qktm[b], off, True), hv(qktm[b], off, True),
                        hv(cstile[b], 0, True), op=OP.mult),
                        reads=[qk, ("cs", b)], writes=[qk])
                    em.emit("v", lambda v, b=b, off=off: v.tensor_tensor(
                        hv(qktm[b], off, False), hv(qktm[b], off, False),
                        hv(cstile[b], 0, False), op=OP.mult),
                        reads=[qk, ("cs", b)], writes=[qk])
                    em.emit("v", lambda v, b=b, off=off: v.tensor_tensor(
                        hv(qktm[b], off, True), hv(qktm[b], off, True),
                        hv(rtmp, 0, True), op=OP.subtract),
                        reads=[qk, ("rtmp", 0)], writes=[qk])
                    em.emit("v", lambda v, b=b, off=off: v.tensor_tensor(
                        hv(qktm[b], off, False), hv(qktm[b], off, False),
                        hv(rtmp, 0, False), op=OP.add),
                        reads=[qk, ("rtmp", 1)], writes=[qk])
            for dst, off, pt in ((qfm, 0, 0), (kfm, 1024, 1)):
                for j in range(8):
                    em.emit("t", lambda t, b=b, j=j, off=off, pt=pt: t.transpose(
                        ptr[pt][:, j * 128:(j + 1) * 128],
                        qktm[b][:, off + j * 128:off + (j + 1) * 128],
                        ident[:, :]),
                        reads=[("qktm", b), "c_ident"], writes=[("ptr", pt)])
                em.emit("v", lambda v, dst=dst, c=c, pt=pt: v.tensor_copy(
                    dst[:, :, c * 128:(c + 1) * 128], ptr[pt][:, :]),
                    reads=[("ptr", pt)], writes=[(dst.name, c)])
        em.barrier()
        ab.close()

        ph2 = ExitStack()
        Ofm = sb(em, ph2, f"{L}_Ofm", [128, 8, S], BF16)
        cc = ExitStack()
        psc = ps(em, cc, f"{L}_psc", [128, 2048], F32)
        ptr2 = [ps(em, cc, f"{L}_p2{i}", [128, 1024], BF16) for i in range(2)]
        po = ps(em, cc, f"{L}_po", [64, 512], F32)
        attn = [sb(em, cc, f"{L}_attn{i}", [128, 2048], BF16) for i in range(2)]
        attn2 = [sb(em, cc, f"{L}_attn2{i}", [128, 2048], BF16) for i in range(2)]
        attnT = [sb(em, cc, f"{L}_attnT{i}", [128, 16, 512], BF16)
                 for i in range(2)]
        ssum = sb(em, cc, f"{L}_ssum", [128, 1], F32)
        rcp = sb(em, cc, f"{L}_rcp", [128, 1], F32)

        for h in range(H):
            j, pofs = h // 2, (h % 2) * 64
            for g in range(NQG):
                kcmax = 4 * (g + 1)
                aT = attnT[g % 2]
                aTkey = ("attnT", g % 2)
                for qsub in range(3):
                    qi = 4 * g + qsub
                    if kcmax - (qi + 1) > 0:
                        em.emit("v", lambda v, aT=aT, qi=qi, qsub=qsub,
                                kcmax=kcmax: v.memset(
                                    aT[:, qi + 1:kcmax,
                                       qsub * 128:(qsub + 1) * 128], 0.0),
                                writes=[aTkey])
                for qsub in range(4):
                    qi = 4 * g + qsub
                    nk = (qi + 1) * 128
                    npc = (nk + 511) // 512
                    a1 = attn[qsub % 2]
                    a1key = ("attn", qsub % 2)
                    a2 = attn2[qsub % 2]
                    a2key = ("attn2", qsub % 2)
                    off = 0
                    while off < nk:
                        w = min(512, nk - off)
                        em.emit("t", lambda t, j=j, pofs=pofs, qi=qi, off=off,
                                w=w: t.matmul(
                                    psc[:, off:off + w],
                                    qfm[pofs:pofs + 64, j,
                                        qi * 128:(qi + 1) * 128],
                                    kfm[pofs:pofs + 64, j, off:off + w],
                                    start=True, stop=True),
                                reads=[(qfm.name, qi)] +
                                      [(kfm.name, kc) for kc in
                                       range(off // 128, (off + w) // 128)],
                                writes=[("psc", off // 512)])
                        off += w
                    em.emit("v", lambda v, qi=qi, nk=nk: v.tensor_tensor(
                        psc[:, qi * 128:nk], psc[:, qi * 128:nk], trim[:, :],
                        op=OP.add),
                        reads=[("psc", qi // 4), "c_trim"],
                        writes=[("psc", qi // 4)])
                    em.emit("s", lambda s_, a1=a1, nk=nk: s_.activation(
                        a1[:, 0:nk], psc[:, 0:nk], AF.Exp, scale=0.125,
                        accum_out=ssum[:, :]),
                        reads=[("psc", i) for i in range(npc)],
                        writes=[a1key, "ssum"])
                    em.emit("v", lambda v: v.reciprocal(rcp[:, :], ssum[:, :]),
                            reads=["ssum"], writes=["rcp"])
                    em.emit("v", lambda v, a1=a1, a2=a2, nk=nk: v.tensor_scalar(
                        a2[:, 0:nk], a1[:, 0:nk], rcp[:, :], None, op0=OP.mult),
                        reads=[a1key, "rcp"], writes=[a2key])
                    for kc8 in range(0, qi + 1, 8):
                        n8 = min(8, qi + 1 - kc8)
                        pt = (kc8 // 8) % 2
                        for u in range(n8):
                            em.emit("t", lambda t, a2=a2, kc8=kc8, u=u, pt=pt:
                                    t.transpose(
                                        ptr2[pt][:, u * 128:(u + 1) * 128],
                                        a2[:, (kc8 + u) * 128:(kc8 + u + 1) * 128],
                                        ident[:, :]),
                                    reads=[a2key, "c_ident"],
                                    writes=[("p2", pt)])
                        em.emit("v", lambda v, aT=aT, kc8=kc8, n8=n8, qsub=qsub,
                                pt=pt: v.tensor_copy(
                                    aT[:, kc8:kc8 + n8,
                                       qsub * 128:(qsub + 1) * 128],
                                    ptr2[pt][:, 0:n8 * 128]),
                                reads=[("p2", pt)], writes=[aTkey])
                for kc in range(kcmax):
                    em.emit("t", lambda t, kc=kc, h=h, aT=aT, kcmax=kcmax:
                            t.matmul(po[:, :], vtm[:, kc, h * 64:(h + 1) * 64],
                                     aT[:, kc, :], start=(kc == 0),
                                     stop=(kc == kcmax - 1)),
                            reads=[aTkey, ("vtm", kc)], writes=[("po",)])
                em.emit("v", lambda v, j=j, pofs=pofs, g=g: v.tensor_copy(
                    Ofm[pofs:pofs + 64, j, g * 512:(g + 1) * 512], po[:, :]),
                    reads=[("po",)], writes=[("Ofm", g, j, pofs)])
        em.barrier()
        cc.close()

        dd = ExitStack()
        wot = sb(em, dd, f"{L}_wot", [128, 8, D], BF16)
        xch2 = [sb(em, dd, f"{L}_x2{i}", [128, D], F32) for i in range(2)]
        xo = [sb(em, dd, f"{L}_xo{i}", [128, D], F32) for i in range(2)]
        pd = [ps(em, dd, f"{L}_pd{i}", [128, 512], F32) for i in range(2)]
        for jj in range(8):
            if layer == 0:
                em.emit("q", lambda q, jj=jj: q.dma_start(
                    out=wot[:, jj, :], in_=wo_b[0:D, jj * 128:(jj + 1) * 128],
                    transpose=True),
                    reads=[f"{L}wob"], writes=[("wot", jj)], dma=True)
            else:
                em.emit("q", lambda q, jj=jj: q.dma_start(
                    out=wot[:, jj, :], in_=wo_b[jj * 128:(jj + 1) * 128, 0:D]),
                    reads=[f"{L}wob"], writes=[("wot", jj)], dma=True)
        for c in range(NC_SEQ):
            b = c % 2
            x2k = ("xch2", b)
            em.emit("q", lambda q, b=b, c=c: q.dma_start(
                out=xch2[b][:, :],
                in_=X_in[row0 + c * 128: row0 + (c + 1) * 128, :]),
                writes=[x2k], dma=True)
            for half in range(2):
                pb = pd[half]
                for jj in range(8):
                    em.emit("t", lambda t, pb=pb, jj=jj, c=c, half=half:
                            t.matmul(pb[:, :], Ofm[:, jj, c * 128:(c + 1) * 128],
                                     wot[:, jj, half * 512:(half + 1) * 512],
                                     start=(jj == 0), stop=(jj == 7)),
                            reads=[("wot", jj)] +
                                  [("Ofm", c // 4, j2, p2) for j2 in range(8)
                                   for p2 in (0, 64)],
                            writes=[("pd", half)])
                em.emit("v", lambda v, pb=pb, b=b, half=half: v.tensor_tensor(
                    xo[b][:, half * 512:(half + 1) * 512], pb[:, :],
                    xch2[b][:, half * 512:(half + 1) * 512], op=OP.add),
                    reads=[("pd", half), x2k], writes=[("xo", b)])
            em.emit("q", lambda q, b=b, c=c: q.dma_start(
                out=X_out[row0 + c * 128: row0 + (c + 1) * 128, :],
                in_=xo[b][:, :]),
                reads=[("xo", b)], writes=[("Xout", s, c)], dma=True)
        em.barrier()
        dd.close()
        ph2.close()

    for s in range(NSEQ):
        one_seq(s)
    ph.close()


def emit_router_phase(em, stack, X_in, xnf, Wr, br, g_out, cnt_out, layer):
    nc = em.nc
    L = f"L{layer}r"
    ident, tris = em.consts["ident"], em.consts["tris"]

    wr_b = nc.dram_tensor(f"{L}_wr_b", [E, D], BF16).ap()
    emit_cast(em, wr_b, Wr, f"{L}wrb")

    rr = ExitStack()
    wr_sb = sb(em, rr, f"{L}_wr", [8, D], BF16)
    wrt = sb(em, rr, f"{L}_wrt", [128, 8, 8], BF16)
    br_sb = sb(em, rr, f"{L}_br", [1, 8], F32)
    ones_f = sb(em, rr, f"{L}_ones", [1, 128], F32)
    xch = [sb(em, rr, f"{L}_xch{i}", [128, D], F32) for i in range(2)]
    xntm = [sb(em, rr, f"{L}_xntm{i}", [128, D], BF16) for i in range(2)]
    xnfm = [sb(em, rr, f"{L}_xnfm{i}", [128, 8, 128], BF16) for i in range(2)]
    lnb = mk_lnb(em, rr, L)
    probs = sb(em, rr, f"{L}_probs", [128, 8], F32)
    m1 = sb(em, rr, f"{L}_m1", [128, 1], F32)
    eq1 = sb(em, rr, f"{L}_eq1", [128, 8], F32)
    pm = sb(em, rr, f"{L}_pm", [128, 8], F32)
    m2 = sb(em, rr, f"{L}_m2", [128, 1], F32)
    keep = sb(em, rr, f"{L}_keep", [128, 8], F32)
    expv = sb(em, rr, f"{L}_expv", [128, 8], F32)
    ssum = sb(em, rr, f"{L}_ssum", [128, 1], F32)
    rcp = sb(em, rr, f"{L}_rcp", [128, 1], F32)
    gsb = sb(em, rr, f"{L}_gsb", [128, NCH, 8], F32)
    msb = sb(em, rr, f"{L}_msb", [128, NCH, 8], F32)
    ptr = ps(em, rr, f"{L}_ptr", [128, 1024], BF16)
    ptw = ps(em, rr, f"{L}_ptw", [128, 64], BF16)
    plg = ps(em, rr, f"{L}_plg", [128, 8], F32)
    pcnt = ps(em, rr, f"{L}_pcnt", [1, NCH * 8], F32)

    em.emit("v", lambda v: v.memset(ones_f[:, :], 1.0), writes=["r_ones"])
    em.emit("q", lambda q: q.dma_start(out=wr_sb[:, :], in_=wr_b),
            reads=[f"{L}wrb"], writes=["r_wr"], dma=True)
    if br is not None:
        em.emit("q", lambda q: q.dma_start(out=br_sb[:, :], in_=br[None, :]),
                writes=["r_br"], dma=True)
    for j in range(8):
        em.emit("t", lambda t, j=j: t.transpose(
            ptw[:, j * 8:(j + 1) * 8], wr_sb[:, j * 128:(j + 1) * 128],
            ident[0:8, 0:8]),
            reads=["r_wr", "c_ident"], writes=[("ptw",)])
    em.emit("v", lambda v: v.tensor_copy(wrt[:, :, :], ptw[:, :]),
            reads=[("ptw",)], writes=["r_wrt"])

    def one_chunk(c):
        b = c % 2
        xk = ("rxch", b)
        em.emit("q", lambda q: q.dma_start(
            out=xch[b][:, :], in_=X_in[c * 128:(c + 1) * 128, :]),
            writes=[xk], dma=True)
        emit_ln_chunk(em, lnb, xch[b], xk, xntm[b][:, :], ("rxntm", b))
        for j in range(8):
            em.emit("t", lambda t, j=j: t.transpose(
                ptr[:, j * 128:(j + 1) * 128],
                xntm[b][:, j * 128:(j + 1) * 128], ident[:, :]),
                reads=[("rxntm", b), "c_ident"], writes=[("rptr",)])
        em.emit("v", lambda v: v.tensor_copy(xnfm[b][:, :, :], ptr[:, :]),
                reads=[("rptr",)], writes=[("rxnfm", b)])
        em.emit("q", lambda q: q.dma_start(
            out=xnf[:, :, c * 128:(c + 1) * 128].rearrange("j p t -> p j t"),
            in_=xnfm[b][:, :, :]),
            reads=[("rxnfm", b)], writes=[("xnf", c)], dma=True)
        for j in range(8):
            em.emit("t", lambda t, j=j: t.matmul(
                plg[:, :], xnfm[b][:, j, :], wrt[:, j, :],
                start=(j == 0), stop=(br is None and j == 7)),
                reads=[("rxnfm", b), "r_wrt"], writes=[("plg",)])
        if br is not None:
            em.emit("t", lambda t: t.matmul(
                plg[:, :], ones_f[:, :], br_sb[:, :], start=False, stop=True),
                reads=["r_ones", "r_br"], writes=[("plg",)])
        em.emit("v", lambda v: v.reduce_max(m1[:, :], plg[:, :], axis=AX.X,
                                            negate=True),
                reads=[("plg",)], writes=["r_m1"])
        em.emit("s", lambda s_: s_.activation(expv[:, :], plg[:, :], AF.Exp,
                                              bias=m1[:, :],
                                              accum_out=ssum[:, :]),
                reads=[("plg",), "r_m1"], writes=["r_expv", "r_ssum"])
        em.emit("v", lambda v: v.reciprocal(rcp[:, :], ssum[:, :]),
                reads=["r_ssum"], writes=["r_rcp"])
        em.emit("v", lambda v: v.tensor_scalar(probs[:, :], expv[:, :],
                                               rcp[:, :], None, op0=OP.mult),
                reads=["r_expv", "r_rcp"], writes=["r_probs"])
        em.emit("v", lambda v: v.reduce_max(m1[:, :], probs[:, :], axis=AX.X),
                reads=["r_probs"], writes=["r_m1"])
        em.emit("v", lambda v: v.tensor_scalar(eq1[:, :], probs[:, :],
                                               m1[:, :], None, op0=OP.is_ge),
                reads=["r_probs", "r_m1"], writes=["r_eq1"])
        em.emit("v", lambda v: v.scalar_tensor_tensor(
            pm[:, :], eq1[:, :], NEG, probs[:, :], op0=OP.mult, op1=OP.add),
            reads=["r_eq1", "r_probs"], writes=["r_pm"])
        em.emit("v", lambda v: v.reduce_max(m2[:, :], pm[:, :], axis=AX.X),
                reads=["r_pm"], writes=["r_m2"])
        em.emit("v", lambda v: v.tensor_scalar(keep[:, :], probs[:, :],
                                               m2[:, :], None, op0=OP.is_ge),
                reads=["r_probs", "r_m2"], writes=["r_keep"])
        if layer == 0:
            em.emit("v", lambda v, c=c: v.tensor_tensor(
                gsb[:, c, :], probs[:, :], keep[:, :], op=OP.mult),
                reads=["r_probs", "r_keep"], writes=[("gsb", c)])
            em.emit("v", lambda v, c=c: v.tensor_copy(msb[:, c, :], keep[:, :]),
                    reads=["r_keep"], writes=[("msb", c)])
        else:
            em.emit("v", lambda v, c=c: v.tensor_tensor(
                msb[:, c, :], probs[:, :], keep[:, :], op=OP.mult),
                reads=["r_probs", "r_keep"], writes=[("msb", c)])
            em.emit("v", lambda v, c=c: v.tensor_reduce(
                ssum[:, :], msb[:, c, :], axis=AX.X, op=OP.add),
                reads=[("msb", c)], writes=["r_ssum"])
            em.emit("v", lambda v: v.reciprocal(rcp[:, :], ssum[:, :]),
                    reads=["r_ssum"], writes=["r_rcp"])
            em.emit("v", lambda v, c=c: v.tensor_scalar(
                gsb[:, c, :], msb[:, c, :], rcp[:, :], None, op0=OP.mult),
                reads=[("msb", c), "r_rcp"], writes=[("gsb", c)])

    for c in range(NCH):
        one_chunk(c)
    em.emit("q", lambda q: q.dma_start(
        out=g_out, in_=gsb[:, :, :]),
        reads=[("gsb", c) for c in range(NCH)], writes=["g_out"], dma=True)
    if layer == 0:
        tot_sb = sb(em, rr, f"{L}_tot", [1, NCH * 8], F32)
        cnt_sb = sb(em, rr, f"{L}_cnt", [1, 8], F32)
        ones_col = sb(em, rr, f"{L}_onesc", [128, 1], F32)
        em.emit("v", lambda v: v.memset(ones_col[:, :], 1.0), writes=["r_onc"])
        em.emit("t", lambda t: t.matmul(
            pcnt[:, :], ones_col[:, :], msb[:, :, :], start=True, stop=True),
            reads=["r_onc"] + [("msb", c) for c in range(NCH)],
            writes=[("pcnt",)])
        em.emit("v", lambda v: v.tensor_copy(tot_sb[:, :], pcnt[:, :]),
                reads=[("pcnt",)], writes=["r_tot"])
        em.emit("v", lambda v: v.tensor_reduce(
            cnt_sb[:, :], tot_sb[:, :].rearrange("p (c e) -> p e c", c=NCH),
            axis=AX.X, op=OP.add),
            reads=["r_tot"], writes=["r_cnt"])
        em.emit("q", lambda q: q.dma_start(out=cnt_out, in_=cnt_sb[:, :]),
                reads=["r_cnt"], writes=["cnt_out"], dma=True)
    em.barrier()
    rr.close()


def emit_kept(em, stack, g_in, allcnt, prefmask, gk_sb):
    nc = em.nc
    tris = em.consts["tris"]
    kk = ExitStack()
    g0s = gk_sb
    W = NCH * 8
    msk = sb(em, kk, "k_msk", [128, W], F32)
    allc = sb(em, kk, "k_allc", [8, 8], F32)
    pref = sb(em, kk, "k_pref", [8, 1], F32)
    onesc = sb(em, kk, "k_onesc", [128, 1], F32)
    onesr = sb(em, kk, "k_onesr", [1, 128], F32)
    tot = sb(em, kk, "k_tot", [1, W], F32)
    totc = sb(em, kk, "k_totc", [NCH, 8], F32)
    coff = sb(em, kk, "k_coff", [1, 8], F32)
    ecc = sb(em, kk, "k_ecc", [NCH, 8], F32)
    eccrow = sb(em, kk, "k_eccrow", [1, W], F32)
    kept = sb(em, kk, "k_kept", [128, W], F32)
    pex = ps(em, kk, "k_pex", [128, W], F32)
    paux = ps(em, kk, "k_paux", [max(NCH, 64), 512], F32)
    em.emit("q", lambda q: q.dma_start(out=g0s[:, :, :], in_=g_in),
            writes=["k_g0"], dma=True)
    em.emit("q", lambda q: q.dma_start(out=allc[:, :], in_=allcnt),
            writes=["k_allc"], dma=True)
    em.emit("q", lambda q: q.dma_start(out=pref[:, :], in_=prefmask),
            writes=["k_pref"], dma=True)
    em.emit("v", lambda v: v.memset(onesc[:, :], 1.0), writes=["k_onc"])
    em.emit("v", lambda v: v.memset(onesr[:, :], 1.0), writes=["k_onr"])
    em.emit("v", lambda v: v.tensor_scalar(
        msk[:, :], g0s[:, :, :], 0.0, None, op0=OP.is_gt),
        reads=["k_g0"], writes=["k_msk"])
    em.emit("t", lambda t: t.matmul(pex[:, :], tris[:, :], msk[:, :],
                                    start=True, stop=False),
            reads=["c_tris", "k_msk"], writes=[("k_pex",)])
    em.emit("t", lambda t: t.matmul(paux[0:1, 0:W], onesc[:, :], msk[:, :],
                                    start=True, stop=True),
            reads=["k_onc", "k_msk"], writes=[("k_paux",)])
    em.emit("v", lambda v: v.tensor_copy(tot[:, :], paux[0:1, 0:W]),
            reads=[("k_paux",)], writes=["k_tot"])
    em.emit("g", lambda g: g.dma_start(
        out=totc[:, :], in_=tot[:, :].rearrange("p (c e) -> p c e", c=NCH)),
        reads=["k_tot"], writes=["k_totc"], dma=True)
    em.emit("t", lambda t: t.matmul(paux[0:1, 0:8], pref[:, :], allc[:, :],
                                    start=True, stop=True),
            reads=["k_pref", "k_allc"], writes=[("k_paux",)])
    em.emit("v", lambda v: v.tensor_copy(coff[:, :], paux[0:1, 0:8]),
            reads=[("k_paux",)], writes=["k_coff"])
    em.emit("t", lambda t: t.matmul(paux[0:NCH, 0:8], tris[0:NCH, 0:NCH],
                                    totc[:, :], start=True, stop=False),
            reads=["c_tris", "k_totc"], writes=[("k_paux",)])
    em.emit("t", lambda t: t.matmul(paux[0:NCH, 0:8], onesr[:, 0:NCH],
                                    coff[:, :], start=False, stop=True),
            reads=["k_onr", "k_coff"], writes=[("k_paux",)])
    em.emit("v", lambda v: v.tensor_copy(ecc[:, :], paux[0:NCH, 0:8]),
            reads=[("k_paux",)], writes=["k_ecc"])
    em.emit("g", lambda g: g.dma_start(
        out=eccrow[:, :].rearrange("p (c e) -> p c e", c=NCH), in_=ecc[:, :]),
        reads=["k_ecc"], writes=["k_eccrow"], dma=True)
    em.emit("t", lambda t: t.matmul(pex[:, :], onesr[:, :], eccrow[:, :],
                                    start=False, stop=True),
            reads=["k_onr", "k_eccrow"], writes=[("k_pex",)])
    em.emit("v", lambda v: v.tensor_scalar(kept[:, :], pex[:, :], CAP, None,
                                           op0=OP.is_lt),
            reads=[("k_pex",)], writes=["k_kept"])
    em.emit("v", lambda v: v.tensor_tensor(
        g0s[:, :, :], g0s[:, :, :],
        kept[:, :].rearrange("p (c e) -> p c e", c=NCH), op=OP.mult),
        reads=["k_g0", "k_kept"], writes=["gk"])
    em.barrier()
    kk.close()


def emit_experts0_phase(em, stack, X_in, X_out, xnf, g_in, allcnt, prefmask,
                        Wi_all, Wo_all):
    nc = em.nc
    L = "L0e"
    wi_b = nc.dram_tensor(f"{L}_wi_b", [E, D, 4096], BF16).ap()
    wo_b = nc.dram_tensor(f"{L}_wo_b", [E, 4096, D], BF16).ap()
    for e in range(E):
        emit_cast(em, wi_b[e, :, :], Wi_all[e, :, :], (f"{L}wib", e))
        emit_cast(em, wo_b[e, :, :], Wo_all[e, :, :], (f"{L}wob", e))

    ee = ExitStack()
    gk = sb(em, ee, f"{L}_gk", [128, NCH, 8], F32)
    emit_kept(em, ee, g_in, allcnt, prefmask, gk)

    for half in range(4):
        r0, r1 = half * (NTOK // 4), (half + 1) * (NTOK // 4)
        em.emit("q", lambda q, r0=r0, r1=r1: q.dma_start(
            out=X_out[r0:r1, :], in_=X_in[r0:r1, :]),
            writes=[("Xo_init", half)], dma=True)

    wi_sb = sb(em, ee, f"{L}_wi", [128, 8, 4096], BF16)
    wo_sb = sb(em, ee, f"{L}_wo", [128, 32, D], BF16)
    xt = [sb(em, ee, f"{L}_xt{i}", [128, 8, 512], BF16) for i in range(2)]
    hm = sb(em, ee, f"{L}_hm", [128, 32, 512], BF16)
    yt = [sb(em, ee, f"{L}_yt{i}", [128, D], F32) for i in range(2)]
    php = [ps(em, ee, f"{L}_php{i}", [128, 512], F32) for i in range(2)]
    pyp = [ps(em, ee, f"{L}_pyp{i}", [128, 512], F32) for i in range(2)]
    NT = NTOK // 512

    def one_expert(e):
        for j in range(8):
            em.emit("q", lambda q, j=j: q.dma_start(
                out=wi_sb[:, j, :], in_=wi_b[e, j * 128:(j + 1) * 128, :]),
                reads=[(f"{L}wib", e)], writes=[("wi", j)], dma=True)
        em.emit("q", lambda q: q.dma_start(
            out=wo_sb[:, :, :],
            in_=wo_b[e, :, :].rearrange("(i p) h -> p i h", p=128)),
            reads=[(f"{L}wob", e)], writes=[("wo",)], dma=True)

        def one_ttile(tt):
            bx = tt % 2
            t0 = tt * 512
            em.emit("q", lambda q: q.dma_start(
                out=xt[bx][:, :, :],
                in_=xnf[:, :, t0:t0 + 512].rearrange("j p t -> p j t")),
                writes=[("xt", bx)], dma=True)
            for ic in range(32):
                pb = php[ic % 2]
                for j in range(8):
                    em.emit("t", lambda t, ic=ic, j=j, pb=pb: t.matmul(
                        pb[:, :], wi_sb[:, j, ic * 128:(ic + 1) * 128],
                        xt[bx][:, j, :], start=(j == 0), stop=(j == 7)),
                        reads=[("wi", j), ("xt", bx)], writes=[("php", ic % 2)])
                eng = "s" if ic % 2 == 0 else "v"
                if eng == "s":
                    em.emit("s", lambda s_, ic=ic, pb=pb: s_.activation(
                        hm[:, ic, :], pb[:, :], AF.Relu),
                        reads=[("php", ic % 2)], writes=[("hm", ic)])
                else:
                    em.emit("v", lambda v, ic=ic, pb=pb: v.tensor_scalar(
                        hm[:, ic, :], pb[:, :], 0.0, None, op0=OP.max),
                        reads=[("php", ic % 2)], writes=[("hm", ic)])
            for tsub in range(4):
                c = tt * 4 + tsub
                by = tsub % 2
                for hh in range(2):
                    pb = pyp[by]
                    for ic in range(32):
                        em.emit("t", lambda t, ic=ic, tsub=tsub, hh=hh, pb=pb:
                                t.matmul(pb[:, :],
                                         hm[:, ic, tsub * 128:(tsub + 1) * 128],
                                         wo_sb[:, ic, hh * 512:(hh + 1) * 512],
                                         start=(ic == 0), stop=(ic == 31)),
                                reads=[("hm", ic), ("wo",)],
                                writes=[("pyp", by)])
                    em.emit("v", lambda v, pb=pb, by=by, hh=hh, c=c: v.tensor_scalar(
                        yt[by][:, hh * 512:(hh + 1) * 512], pb[:, :],
                        gk[:, c, e:e + 1], None, op0=OP.mult),
                        reads=[("pyp", by), "gk"], writes=[("yt", by)])
                em.emit("g", lambda g, by=by, c=c: g.dma_start(
                    out=X_out[c * 128:(c + 1) * 128, :], in_=yt[by][:, :],
                    accum_op=OP.add, max_dma_last_dim=4096),
                    reads=[("yt", by), ("Xo_init", (c * 128) // (NTOK // 4))],
                    writes=[("Xo", c)], dma=True)

        for tt in range(NT):
            one_ttile(tt)

    for e in range(E):
        one_expert(e)
    em.barrier()
    ee.close()


def emit_moe1_phase(em, stack, X_in, X_out, xnf, g_in, gup_all, dn_all):
    nc = em.nc
    L = "L1e"
    gup_b = nc.dram_tensor(f"{L}_gup_b", [E, 8192, D], BF16).ap()
    dn_b = nc.dram_tensor(f"{L}_dn_b", [E, D, 4096], BF16).ap()
    for e in range(E):
        emit_cast(em, gup_b[e, :, :], gup_all[e, :, :], (f"{L}gupb", e))
        emit_cast(em, dn_b[e, :, :], dn_all[e, :, :], (f"{L}dnb", e))

    ee = ExitStack()
    g1 = sb(em, ee, f"{L}_g1", [128, NCH, 8], F32)
    em.emit("q", lambda q: q.dma_start(out=g1[:, :, :], in_=g_in),
            writes=["g1"], dma=True)

    for half in range(4):
        r0, r1 = half * (NTOK // 4), (half + 1) * (NTOK // 4)
        em.emit("q", lambda q, r0=r0, r1=r1: q.dma_start(
            out=X_out[r0:r1, :], in_=X_in[r0:r1, :]),
            writes=[("Xo_init", half)], dma=True)

    gt = sb(em, ee, f"{L}_gt", [128, 8, 2048], BF16)
    ut = sb(em, ee, f"{L}_ut", [128, 8, 2048], BF16)
    dt = sb(em, ee, f"{L}_dt", [128, 16, D], BF16)
    xt = [sb(em, ee, f"{L}_xt{i}", [128, 8, 512], BF16) for i in range(2)]
    ha = sb(em, ee, f"{L}_ha", [128, 16, 512], BF16)
    sg = sb(em, ee, f"{L}_sg", [128, 512], F32)
    yt = [sb(em, ee, f"{L}_yt{i}", [128, D], F32) for i in range(2)]
    pg = ps(em, ee, f"{L}_pg", [128, 512], F32)
    pu = ps(em, ee, f"{L}_pu", [128, 512], F32)
    pyp = [ps(em, ee, f"{L}_pyp{i}", [128, 512], F32) for i in range(2)]
    NT = NTOK // 512

    def one_half(e, ih):
        for j in range(8):
            em.emit("q", lambda q, j=j: q.dma_start(
                out=gt[:, j, :],
                in_=gup_b[e, ih * 2048:(ih + 1) * 2048, j * 128:(j + 1) * 128],
                transpose=True),
                reads=[(f"{L}gupb", e)], writes=[("gt", j)], dma=True)
            em.emit("q", lambda q, j=j: q.dma_start(
                out=ut[:, j, :],
                in_=gup_b[e, 4096 + ih * 2048:4096 + (ih + 1) * 2048,
                          j * 128:(j + 1) * 128],
                transpose=True),
                reads=[(f"{L}gupb", e)], writes=[("ut", j)], dma=True)
        for ic in range(16):
            em.emit("q", lambda q, ic=ic: q.dma_start(
                out=dt[:, ic, :],
                in_=dn_b[e, :, ih * 2048 + ic * 128: ih * 2048 + (ic + 1) * 128],
                transpose=True),
                reads=[(f"{L}dnb", e)], writes=[("dt", ic)], dma=True)

        def one_ttile(tt):
            bx = tt % 2
            t0 = tt * 512
            em.emit("q", lambda q: q.dma_start(
                out=xt[bx][:, :, :],
                in_=xnf[:, :, t0:t0 + 512].rearrange("j p t -> p j t")),
                writes=[("xt", bx)], dma=True)
            for oc in range(16):
                for j in range(8):
                    em.emit("t", lambda t, oc=oc, j=j: t.matmul(
                        pg[:, :], gt[:, j, oc * 128:(oc + 1) * 128],
                        xt[bx][:, j, :], start=(j == 0), stop=(j == 7)),
                        reads=[("gt", j), ("xt", bx)], writes=[("pg",)])
                for j in range(8):
                    em.emit("t", lambda t, oc=oc, j=j: t.matmul(
                        pu[:, :], ut[:, j, oc * 128:(oc + 1) * 128],
                        xt[bx][:, j, :], start=(j == 0), stop=(j == 7)),
                        reads=[("ut", j), ("xt", bx)], writes=[("pu",)])
                em.emit("s", lambda s_, oc=oc: s_.activation(
                    sg[:, :], pg[:, :], AF.Silu),
                    reads=[("pg",)], writes=["sg"])
                em.emit("v", lambda v, oc=oc: v.tensor_tensor(
                    ha[:, oc, :], sg[:, :], pu[:, :], op=OP.mult),
                    reads=["sg", ("pu",)], writes=[("ha", oc)])
            for tsub in range(4):
                c = tt * 4 + tsub
                by = tsub % 2
                for hh in range(2):
                    pb = pyp[by]
                    for ic in range(16):
                        em.emit("t", lambda t, ic=ic, tsub=tsub, hh=hh, pb=pb:
                                t.matmul(pb[:, :],
                                         ha[:, ic, tsub * 128:(tsub + 1) * 128],
                                         dt[:, ic, hh * 512:(hh + 1) * 512],
                                         start=(ic == 0), stop=(ic == 15)),
                                reads=[("ha", ic), ("dt", ic)],
                                writes=[("pyp", by)])
                    em.emit("v", lambda v, pb=pb, by=by, hh=hh, c=c:
                            v.tensor_scalar(
                                yt[by][:, hh * 512:(hh + 1) * 512], pb[:, :],
                                g1[:, c, e:e + 1], None, op0=OP.mult),
                            reads=[("pyp", by), "g1"], writes=[("yt", by)])
                em.emit("g", lambda g, by=by, c=c: g.dma_start(
                    out=X_out[c * 128:(c + 1) * 128, :], in_=yt[by][:, :],
                    accum_op=OP.add, max_dma_last_dim=4096),
                    reads=[("yt", by), ("Xo_init", (c * 128) // (NTOK // 4))],
                    writes=[("Xo", c)], dma=True)

        for tt in range(NT):
            one_ttile(tt)

    for e in range(E):
        for ih in range(2):
            one_half(e, ih)
    em.barrier()
    ee.close()


def build_attn_program(layer):
    nc = bass.Bass("TRN2", target_bir_lowering=False)
    X_in = nc.dram_tensor("X", [NTOK, D], F32, kind="ExternalInput").ap()
    Wq = nc.dram_tensor("Wq", [D, D], F32, kind="ExternalInput").ap()
    Wk = nc.dram_tensor("Wk", [D, D], F32, kind="ExternalInput").ap()
    Wv = nc.dram_tensor("Wv", [D, D], F32, kind="ExternalInput").ap()
    Wo = nc.dram_tensor("Wo", [D, D], F32, kind="ExternalInput").ap()
    if layer == 0:
        rc = nc.dram_tensor("rope_cos", [S, D], BF16, kind="ExternalInput").ap()
        rs_ = nc.dram_tensor("rope_sin", [S, D], BF16, kind="ExternalInput").ap()
    else:
        rc = rs_ = None
    X_out = nc.dram_tensor("Xo", [NTOK, D], F32, kind="ExternalOutput").ap()
    em = Em(nc)
    cstack = ExitStack()
    emit_consts(em, cstack)
    emit_attn_phase(em, cstack, X_in, X_out, Wq, Wk, Wv, Wo, rc, rs_, layer)
    em.finalize()
    cstack.close()
    return nc


def build_router_program(layer):
    nc = bass.Bass("TRN2", target_bir_lowering=False)
    X_in = nc.dram_tensor("X", [NTOK, D], F32, kind="ExternalInput").ap()
    Wr = nc.dram_tensor("Wr", [E, D], F32, kind="ExternalInput").ap()
    br = (nc.dram_tensor("br", [E], F32, kind="ExternalInput").ap()
          if layer == 0 else None)
    xnf = nc.dram_tensor("xnf", [8, 128, NTOK], BF16, kind="ExternalOutput").ap()
    g_out = nc.dram_tensor("g", [128, NCH * 8], F32, kind="ExternalOutput").ap()
    cnt = (nc.dram_tensor("cnt", [1, E], F32, kind="ExternalOutput").ap()
           if layer == 0 else None)
    em = Em(nc)
    cstack = ExitStack()
    emit_consts(em, cstack)
    emit_router_phase(em, cstack, X_in, xnf, Wr, br, g_out, cnt, layer)
    em.finalize()
    cstack.close()
    return nc


def build_experts0_program():
    nc = bass.Bass("TRN2", target_bir_lowering=False)
    X_in = nc.dram_tensor("X", [NTOK, D], F32, kind="ExternalInput").ap()
    xnf = nc.dram_tensor("xnf", [8, 128, NTOK], BF16, kind="ExternalInput").ap()
    g_in = nc.dram_tensor("g", [128, NCH * 8], F32, kind="ExternalInput").ap()
    allcnt = nc.dram_tensor("allcnt", [NCORES, E], F32, kind="ExternalInput").ap()
    prefmask = nc.dram_tensor("prefmask", [NCORES, 1], F32, kind="ExternalInput").ap()
    Wi = nc.dram_tensor("Wi", [E, D, 4096], F32, kind="ExternalInput").ap()
    Wo = nc.dram_tensor("Wo", [E, 4096, D], F32, kind="ExternalInput").ap()
    X_out = nc.dram_tensor("Xo", [NTOK, D], F32, kind="ExternalOutput").ap()
    em = Em(nc)
    cstack = ExitStack()
    emit_consts(em, cstack)
    emit_experts0_phase(em, cstack, X_in, X_out, xnf, g_in.rearrange(
        "p (c e) -> p c e", e=E), allcnt, prefmask, Wi, Wo)
    em.finalize()
    cstack.close()
    return nc


def build_moe1_program():
    nc = bass.Bass("TRN2", target_bir_lowering=False)
    X_in = nc.dram_tensor("X", [NTOK, D], F32, kind="ExternalInput").ap()
    Wr = nc.dram_tensor("Wr", [E, D], F32, kind="ExternalInput").ap()
    gup = nc.dram_tensor("gup", [E, 8192, D], F32, kind="ExternalInput").ap()
    dn = nc.dram_tensor("dn", [E, D, 4096], F32, kind="ExternalInput").ap()
    X_out = nc.dram_tensor("Xo", [NTOK, D], F32, kind="ExternalOutput").ap()
    xnf = nc.dram_tensor("xnf1", [8, 128, NTOK], BF16).ap()
    g_sc = nc.dram_tensor("g1_sc", [128, NCH * 8], F32).ap()
    em = Em(nc)
    cstack = ExitStack()
    emit_consts(em, cstack)
    emit_router_phase(em, cstack, X_in, xnf, Wr, None, g_sc, None, 1)
    emit_moe1_phase(em, cstack, X_in, X_out, xnf, g_sc, gup, dn)
    em.finalize()
    cstack.close()
    return nc


def _rope_tables():
    inv = 1.0 / (10000.0 ** (np.arange(0, DH, 2, dtype=np.float64) / DH))
    t = np.arange(S, dtype=np.float64)
    freqs = np.outer(t, inv)
    emb = np.concatenate([freqs, freqs], axis=1)
    cos = np.cos(emb).astype(np.float32)
    sin = np.sin(emb).astype(np.float32)
    return (np.tile(cos, (1, H)).astype(ml_dtypes.bfloat16).copy(),
            np.tile(sin, (1, H)).astype(ml_dtypes.bfloat16).copy())


_PROG_CACHE = {}


def _get_prog(key, builder):
    if key not in _PROG_CACHE:
        _PROG_CACHE[key] = builder()
    return _PROG_CACHE[key]



def build_fused_program():
    nc = bass.Bass("TRN2", target_bir_lowering=False)
    x_in = nc.dram_tensor("x", [NTOK, D], F32, kind="ExternalInput").ap()
    Wq0 = nc.dram_tensor("Wq0", [D, D], F32, kind="ExternalInput").ap()
    Wk0 = nc.dram_tensor("Wk0", [D, D], F32, kind="ExternalInput").ap()
    Wv0 = nc.dram_tensor("Wv0", [D, D], F32, kind="ExternalInput").ap()
    Wo0 = nc.dram_tensor("Wo0", [D, D], F32, kind="ExternalInput").ap()
    rc = nc.dram_tensor("rope_cos", [S, D], BF16, kind="ExternalInput").ap()
    rs_ = nc.dram_tensor("rope_sin", [S, D], BF16, kind="ExternalInput").ap()
    Wr0 = nc.dram_tensor("Wr0", [E, D], F32, kind="ExternalInput").ap()
    br0 = nc.dram_tensor("br0", [E], F32, kind="ExternalInput").ap()
    Wi0 = nc.dram_tensor("Wi0", [E, D, 4096], F32, kind="ExternalInput").ap()
    Wo0e = nc.dram_tensor("Wo0e", [E, 4096, D], F32, kind="ExternalInput").ap()
    Wq1 = nc.dram_tensor("Wq1", [D, D], F32, kind="ExternalInput").ap()
    Wk1 = nc.dram_tensor("Wk1", [D, D], F32, kind="ExternalInput").ap()
    Wv1 = nc.dram_tensor("Wv1", [D, D], F32, kind="ExternalInput").ap()
    Wo1 = nc.dram_tensor("Wo1", [D, D], F32, kind="ExternalInput").ap()
    Wr1 = nc.dram_tensor("Wr1", [E, D], F32, kind="ExternalInput").ap()
    gup = nc.dram_tensor("gup", [E, 8192, D], F32, kind="ExternalInput").ap()
    dn = nc.dram_tensor("dn", [E, D, 4096], F32, kind="ExternalInput").ap()
    onehot = nc.dram_tensor("onehot", [1, NCORES], F32, kind="ExternalInput").ap()
    prefmask = nc.dram_tensor("prefmask", [NCORES, 1], F32, kind="ExternalInput").ap()
    out = nc.dram_tensor("out", [NTOK, D], F32, kind="ExternalOutput").ap()

    dbg = "ExternalOutput" if DEBUG_X else "Internal"
    X1 = nc.dram_tensor("X1", [NTOK, D], F32, kind=dbg).ap()
    X2 = nc.dram_tensor("X2", [NTOK, D], F32, kind=dbg).ap()
    X3 = nc.dram_tensor("X3", [NTOK, D], F32, kind=dbg).ap()
    xnf0 = nc.dram_tensor("xnf0", [8, 128, NTOK], BF16).ap()
    xnf1 = nc.dram_tensor("xnf1", [8, 128, NTOK], BF16).ap()
    g0 = nc.dram_tensor("g0_sc", [128, NCH * 8], F32).ap()
    g1 = nc.dram_tensor("g1_sc", [128, NCH * 8], F32).ap()
    cnt_loc = nc.dram_tensor("cnt_loc", [1, E], F32).ap()
    cc_in_t = nc.dram_tensor("cc_in", [NCORES, E], F32)
    cc_out_t = nc.dram_tensor("cc_out", [NCORES, E], F32)

    em = Em(nc)
    cstack = ExitStack()
    emit_consts(em, cstack)

    emit_attn_phase(em, cstack, x_in, X1, Wq0, Wk0, Wv0, Wo0, rc, rs_, 0)
    emit_router_phase(em, cstack, X1, xnf0, Wr0, br0, g0, cnt_loc, 0)

    ex = ExitStack()
    oh_sb = sb(em, ex, "cc_oh", [1, NCORES], F32)
    cl_sb = sb(em, ex, "cc_cl", [1, E], F32)
    ci_sb = sb(em, ex, "cc_ci", [NCORES, E], F32)
    pcc = ps(em, ex, "cc_p", [NCORES, E], F32)
    em.emit("q", lambda q: q.dma_start(out=oh_sb[:, :], in_=onehot),
            writes=["cc_oh"], dma=True)
    em.emit("q", lambda q: q.dma_start(out=cl_sb[:, :], in_=cnt_loc),
            reads=["cnt_out"], writes=["cc_cl"], dma=True)
    em.emit("t", lambda t: t.matmul(pcc[:, :], oh_sb[:, :], cl_sb[:, :],
                                    start=True, stop=True),
            reads=["cc_oh", "cc_cl"], writes=[("cc_p",)])
    em.emit("v", lambda v: v.tensor_copy(ci_sb[:, :], pcc[:, :]),
            reads=[("cc_p",)], writes=["cc_ci"])
    em.emit("q", lambda q: q.dma_start(out=cc_in_t.ap(), in_=ci_sb[:, :]),
            reads=["cc_ci"], writes=["cc_in_d"], dma=True)
    em.barrier(engines=("g",))
    if SIM_CC_COPY:
        em.emit("g", lambda g: g.dma_start(out=cc_out_t.ap(), in_=cc_in_t.ap()),
                reads=["cc_in_d"], writes=["cc_out_d"], dma=True)
    else:
        em.emit("g", lambda g: g.collective_compute(
            "AllReduce", OP.add, replica_groups=[list(range(NCORES))],
            ins=[cc_in_t.ap().opt()], outs=[cc_out_t.ap().opt()]),
            reads=["cc_in_d"], writes=["cc_out_d"])
    em.barrier()
    ex.close()

    emit_experts0_phase(em, cstack, X1, X2, xnf0, g0.rearrange(
        "p (c e) -> p c e", e=E), cc_out_t.ap(), prefmask, Wi0, Wo0e)
    emit_attn_phase(em, cstack, X2, X3, Wq1, Wk1, Wv1, Wo1, None, None, 1)
    emit_router_phase(em, cstack, X3, xnf1, Wr1, None, g1, None, 1)
    emit_moe1_phase(em, cstack, X3, out, xnf1, g1, gup, dn)
    em.finalize()
    cstack.close()
    return nc


def kernel(**inputs):
    xs = np.ascontiguousarray(np.asarray(inputs["L0_attn_x"], dtype=np.float32)
                              .reshape(NCORES, NTOK, D))
    cos, sin = _rope_tables()
    cores = list(range(NCORES))
    wq1 = np.ascontiguousarray(np.asarray(inputs["L1_attn_w_q"],
                                          dtype=np.float32).reshape(D, D))
    wk1 = np.ascontiguousarray(np.asarray(inputs["L1_attn_w_k"],
                                          dtype=np.float32).reshape(D, D))
    wv1 = np.ascontiguousarray(np.asarray(inputs["L1_attn_w_v"],
                                          dtype=np.float32).reshape(D, D))
    nc = _get_prog("fused", build_fused_program)
    im = []
    for c in cores:
        oh = np.zeros((1, NCORES), np.float32)
        oh[0, c] = 1.0
        pref = np.zeros((NCORES, 1), np.float32)
        pref[:c, 0] = 1.0
        im.append(dict(
            x=xs[c], Wq0=inputs["L0_attn_Wq"], Wk0=inputs["L0_attn_Wk"],
            Wv0=inputs["L0_attn_Wv"], Wo0=inputs["L0_attn_Wo"],
            rope_cos=cos, rope_sin=sin,
            Wr0=inputs["L0_ffn_W_router"], br0=inputs["L0_ffn_b_router"],
            Wi0=inputs["L0_ffn_experts_inter"], Wo0e=inputs["L0_ffn_experts_out"],
            Wq1=wq1, Wk1=wk1, Wv1=wv1, Wo1=inputs["L1_attn_w_o"],
            Wr1=inputs["L1_ffn_gate_w"], gup=inputs["L1_ffn_gate_up_proj"],
            dn=inputs["L1_ffn_down_proj"], onehot=oh, prefmask=pref))
    r = run_bass_kernel_spmd(nc, im, core_ids=cores)
    return np.stack([r.results[c]["out"] for c in cores]).reshape(B, S, D)


def kernel_multi(**inputs):
    xs = np.ascontiguousarray(np.asarray(inputs["L0_attn_x"], dtype=np.float32)
                              .reshape(NCORES, NTOK, D))
    cos, sin = _rope_tables()
    cores = list(range(NCORES))

    nc1 = _get_prog("attn0", lambda: build_attn_program(0))
    im = [dict(X=xs[c], Wq=inputs["L0_attn_Wq"], Wk=inputs["L0_attn_Wk"],
               Wv=inputs["L0_attn_Wv"], Wo=inputs["L0_attn_Wo"],
               rope_cos=cos, rope_sin=sin) for c in cores]
    r = run_bass_kernel_spmd(nc1, im, core_ids=cores)
    X = [r.results[c]["Xo"] for c in cores]

    nc2 = _get_prog("router0", lambda: build_router_program(0))
    im = [dict(X=X[c], Wr=inputs["L0_ffn_W_router"],
               br=inputs["L0_ffn_b_router"]) for c in cores]
    r = run_bass_kernel_spmd(nc2, im, core_ids=cores)
    xnf = [r.results[c]["xnf"] for c in cores]
    g0 = [r.results[c]["g"] for c in cores]
    allcnt = np.concatenate([r.results[c]["cnt"] for c in cores], axis=0)
    allcnt = np.ascontiguousarray(allcnt.astype(np.float32))

    nc3 = _get_prog("experts0", build_experts0_program)
    pref = np.zeros((NCORES, NCORES, 1), np.float32)
    for c in cores:
        pref[c, :c, 0] = 1.0
    im = [dict(X=X[c], xnf=xnf[c], g=g0[c], allcnt=allcnt,
               prefmask=pref[c], Wi=inputs["L0_ffn_experts_inter"],
               Wo=inputs["L0_ffn_experts_out"]) for c in cores]
    r = run_bass_kernel_spmd(nc3, im, core_ids=cores)
    X = [r.results[c]["Xo"] for c in cores]

    nc4 = _get_prog("attn1", lambda: build_attn_program(1))
    wq1 = np.ascontiguousarray(np.asarray(inputs["L1_attn_w_q"],
                                          dtype=np.float32).reshape(D, D))
    wk1 = np.ascontiguousarray(np.asarray(inputs["L1_attn_w_k"],
                                          dtype=np.float32).reshape(D, D))
    wv1 = np.ascontiguousarray(np.asarray(inputs["L1_attn_w_v"],
                                          dtype=np.float32).reshape(D, D))
    im = [dict(X=X[c], Wq=wq1, Wk=wk1, Wv=wv1,
               Wo=inputs["L1_attn_w_o"]) for c in cores]
    r = run_bass_kernel_spmd(nc4, im, core_ids=cores)
    X = [r.results[c]["Xo"] for c in cores]

    nc5 = _get_prog("moe1", build_moe1_program)
    im = [dict(X=X[c], Wr=inputs["L1_ffn_gate_w"],
               gup=inputs["L1_ffn_gate_up_proj"],
               dn=inputs["L1_ffn_down_proj"]) for c in cores]
    r = run_bass_kernel_spmd(nc5, im, core_ids=cores)
    out = np.stack([r.results[c]["Xo"] for c in cores]).reshape(B, S, D)
    return out


# version 04 (final)
# speedup vs baseline: 1.1088x; 1.1088x over previous
from contextlib import ExitStack

import numpy as np
import ml_dtypes
import concourse.bass as bass
import concourse.mybir as mybir
from concourse.bass_utils import run_bass_kernel_spmd

F32 = mybir.dt.float32
BF16 = mybir.dt.bfloat16
I32 = mybir.dt.int32
AF = mybir.ActivationFunctionType
OP = mybir.AluOpType
AX = mybir.AxisListType

NCORES = 8
B, S, D = 32, 2048, 1024
NSEQ = B // NCORES
NTOK = NSEQ * S
NCH = NTOK // 128
H, DH = 16, 64
E = 8
CAP = 10240.0
EPS = 1e-5
NEG = -1.0e30
NC_SEQ = S // 128
SIM_CC_COPY = False
DEBUG_X = False
NQG = S // 512


class Em:

    ALL = ("t", "v", "s", "g", "q")
    NWAVE = 6

    def __init__(self, nc):
        self.nc = nc
        self.es = ExitStack()
        self.sem = {}
        for name in ("t", "v", "s", "g"):
            self.sem[name] = self.es.enter_context(nc.semaphore("sem_" + name))
        for qe in ("q", "g", "s"):
            for i in range(Em.NWAVE):
                name = f"d{qe}{i}"
                self.sem[name] = self.es.enter_context(nc.semaphore("sem_" + name))
        self.cnt = {k: 0 for k in self.sem}
        self.ops = {e: [] for e in Em.ALL}
        self.lw = {}
        self.rd = {}
        self.wm = {e: {sn: 0 for sn in self.sem} for e in Em.ALL}
        self.consts = None
        self.wrot = {qe: 0 for qe in ("q", "g", "s")}
        self.wcur = {qe: None for qe in ("q", "g", "s")}
        self.waves = {}
        self._wid = 0

    def _seal(self, qe):
        w = self.wcur[qe]
        if w is None:
            return
        self.waves[w["id"]] = self.cnt[w["sem"]]
        self.wcur[qe] = None

    def seal_all(self):
        for qe in ("q", "g", "s"):
            self._seal(qe)

    def _resolve(self, dep):
        sn, c = dep
        if isinstance(c, tuple) and c[0] == "wave":
            wid, qe = c[1], c[2]
            if wid not in self.waves:
                self._seal(qe)
            return (sn, self.waves[wid])
        return (sn, c)

    def emit(self, e, fn, reads=(), writes=(), dma=False):
        deps = {}

        def add(sn, c):
            if c > deps.get(sn, 0):
                deps[sn] = c

        for k in reads:
            if k in self.lw:
                add(*self._resolve(self.lw[k]))
        for k in writes:
            if k in self.lw:
                add(*self._resolve(self.lw[k]))
            for s2, c2 in self.rd.get(k, {}).items():
                add(*self._resolve((s2, c2)))

        pre_gate = None
        if dma:
            qe = e
            w = self.wcur[qe]
            if w is None:
                idx = self.wrot[qe]
                self.wrot[qe] = (idx + 1) % Em.NWAVE
                semname = f"d{qe}{idx}"
                self._wid += 1
                w = dict(id=self._wid, sem=semname)
                self.wcur[qe] = w
                if self.cnt[semname] > 0:
                    pre_gate = (semname, self.cnt[semname])
                    self.wm[e][semname] = max(self.wm[e][semname],
                                              self.cnt[semname])
            semname = w["sem"]
            marker = ("wave", w["id"], qe)
        else:
            semname = e
            marker = None

        waits = []
        for s2, c2 in sorted(deps.items(), key=str):
            if e == "t" and s2 == "t":
                continue
            if self.wm[e][s2] >= c2:
                continue
            assert c2 <= self.cnt[s2], f"future wait {s2}={c2}>{self.cnt[s2]}"
            waits.append((s2, c2))
            self.wm[e][s2] = c2
        if pre_gate is not None:
            waits.insert(0, pre_gate)
        inc = 16 if dma else 1
        self.cnt[semname] += inc
        mycnt = self.cnt[semname]
        sems = self.sem

        def run(engobj, waits=waits, fn=fn, sname=semname, inc=inc):
            for s2, c2 in waits:
                engobj.wait_ge(sems[s2], c2)
            fn(engobj).then_inc(sems[sname], inc)

        self.ops[e].append(run)
        lwval = (semname, marker if dma else mycnt)
        for k in writes:
            self.lw[k] = lwval
            self.rd[k] = {}
        for k in reads:
            d = self.rd.setdefault(k, {})
            d[semname] = marker if dma else mycnt
        return lwval

    def barrier(self, engines=ALL):
        self.seal_all()
        snap = dict(self.cnt)
        sems = self.sem
        for e in engines:
            waits = []
            for sn, c in sorted(snap.items()):
                if c > 0 and self.wm[e][sn] < c:
                    waits.append((sn, c))
                    self.wm[e][sn] = c
            if not waits:
                continue

            def run(engobj, waits=waits):
                for sn, c in waits:
                    engobj.wait_ge(sems[sn], c)

            self.ops[e].append(run)

    def finalize(self):
        nc = self.nc
        self.barrier(engines=("q",))
        with nc.Block() as block:
            for ename, dec in (("t", block.tensor), ("v", block.vector),
                               ("s", block.scalar), ("g", block.gpsimd),
                               ("q", block.sync)):
                ops = self.ops[ename]

                def body(engobj, ops=ops):
                    for op in ops:
                        op(engobj)

                dec(body)
        self.es.close()


_UID = [0]


def _uname(name):
    _UID[0] += 1
    return f"{name}_u{_UID[0]}"


def sb(em, stack, name, shape, dtype):
    return stack.enter_context(em.nc.sbuf_tensor(_uname(name), shape, dtype))


def ps(em, stack, name, shape, dtype):
    return stack.enter_context(em.nc.psum_tensor(_uname(name), shape, dtype))


def emit_consts(em, stack):
    ii = sb(em, stack, "c_ii", [128, 128], I32)
    ident = sb(em, stack, "c_ident", [128, 128], BF16)
    tris = sb(em, stack, "c_tris", [128, 128], F32)
    em.emit("g", lambda g: g.iota(ii[:, :], pattern=[[1, 128]], base=0,
                                  channel_multiplier=-1), writes=["c_ii"])
    em.emit("v", lambda v: v.tensor_scalar(ident[:, :], ii[:, :], 0, None,
                                           op0=OP.is_equal),
            reads=["c_ii"], writes=["c_ident"])
    em.emit("v", lambda v: v.tensor_scalar(tris[:, :], ii[:, :], 0, None,
                                           op0=OP.is_gt),
            reads=["c_ii"], writes=["c_tris"])
    trimg = sb(em, stack, "c_trimg", [128, 4, 512], BF16)
    for kcl in range(4):
        for qb in range(4):
            if qb < kcl:
                em.emit("v", lambda v, kcl=kcl, qb=qb: v.memset(
                    trimg[:, kcl, qb * 128:(qb + 1) * 128], NEG),
                    writes=["c_trimg"])
            elif qb == kcl:
                em.emit("v", lambda v, kcl=kcl, qb=qb: v.tensor_scalar(
                    trimg[:, kcl, qb * 128:(qb + 1) * 128], ii[:, :], 0, NEG,
                    op0=OP.is_lt, op1=OP.mult),
                    reads=["c_ii"], writes=["c_trimg"])
            else:
                em.emit("v", lambda v, kcl=kcl, qb=qb: v.memset(
                    trimg[:, kcl, qb * 128:(qb + 1) * 128], 0.0),
                    writes=["c_trimg"])
    eps = sb(em, stack, "c_eps", [128, 1], F32)
    em.emit("v", lambda v: v.memset(eps[:, :], EPS), writes=["c_eps"])
    em.consts = dict(ident=ident, tris=tris, eps=eps, trimg=trimg, ii=ii)


def emit_cast(em, dst_ap, src_ap, key):
    em.emit("g", lambda g: g.dma_start(out=dst_ap, in_=src_ap,
                                       max_dma_last_dim=8192),
            writes=[key], dma=True)


def emit_ln_chunk(em, lnb, x_sb, xkey, xn_ap, xnkey):
    bnst, mv, sdv, rstd = lnb["bnst"], lnb["mv"], lnb["sdv"], lnb["rstd"]
    for i in range(2):
        em.emit("v", lambda v, i=i: v.bn_stats(bnst[:, 6 * i:6 * i + 6],
                                               x_sb[:, 512 * i:512 * (i + 1)]),
                reads=[xkey], writes=["ln_bnst"])
    em.emit("v", lambda v: v.bn_aggr(mv[:, :], bnst[:, :]),
            reads=["ln_bnst"], writes=["ln_mv"])
    epsv = em.consts["eps"]
    em.emit("s", lambda s_: s_.activation(sdv[:, :], mv[:, 1:2], AF.Sqrt,
                                          bias=epsv[:, :]),
            reads=["ln_mv", "c_eps"], writes=["ln_sdv"])
    em.emit("v", lambda v: v.reciprocal(rstd[:, :], sdv[:, :]),
            reads=["ln_sdv"], writes=["ln_rstd"])
    em.emit("v", lambda v: v.tensor_scalar(xn_ap, x_sb[:, :], mv[:, 0:1],
                                           rstd[:, :], op0=OP.subtract,
                                           op1=OP.mult),
            reads=[xkey, "ln_mv", "ln_rstd"], writes=[xnkey])


def mk_lnb(em, stack, tag):
    return dict(
        bnst=sb(em, stack, f"{tag}_bnst", [128, 12], F32),
        mv=sb(em, stack, f"{tag}_mv", [128, 2], F32),
        sdv=sb(em, stack, f"{tag}_sdv", [128, 1], F32),
        rstd=sb(em, stack, f"{tag}_rstd", [128, 1], F32),
    )


def emit_attn_phase(em, stack, X_in, X_out, Wq, Wk, Wv, Wo, rope_cos, rope_sin,
                    layer):
    nc = em.nc
    L = f"L{layer}a"
    ident, trimg = em.consts["ident"], em.consts["trimg"]

    wq_b = nc.dram_tensor(f"{L}_wq_b", [D, D], BF16).ap()
    wk_b = nc.dram_tensor(f"{L}_wk_b", [D, D], BF16).ap()
    wv_b = nc.dram_tensor(f"{L}_wv_b", [D, D], BF16).ap()
    emit_cast(em, wq_b, Wq, f"{L}wqb")
    emit_cast(em, wk_b, Wk, f"{L}wkb")
    emit_cast(em, wv_b, Wv, f"{L}wvb")
    if layer == 0:
        wo_b = nc.dram_tensor(f"{L}_wo_b", [D, D], BF16).ap()
        emit_cast(em, wo_b, Wo, f"{L}wob")

    ph = ExitStack()
    qfm = sb(em, ph, f"{L}_qfm", [128, 8, S], BF16)
    kfm = sb(em, ph, f"{L}_kfm", [128, 8, S], BF16)
    vtm = sb(em, ph, f"{L}_vtm", [128, NC_SEQ, H, DH + 1], BF16)
    lnb = mk_lnb(em, ph, L)
    em.emit("v", lambda v: v.memset(vtm[:, :, :, :], 1.0), writes=["vtm_ones"])

    def hv(x, off, lo):
        r = x[:, off:off + 1024].rearrange("p (h x) -> p h x", h=16)
        return r[:, :, 0:32] if lo else r[:, :, 32:64]

    def one_seq(s):
        row0 = s * S
        ab = ExitStack()
        wqt = sb(em, ab, f"{L}_wqt", [128, 8, D], BF16)
        wkt = sb(em, ab, f"{L}_wkt", [128, 8, D], BF16)
        wvt = sb(em, ab, f"{L}_wvt", [128, 8, D], BF16)
        xch = [sb(em, ab, f"{L}_xch{i}", [128, D], F32) for i in range(2)]
        xntm = [sb(em, ab, f"{L}_xntm{i}", [128, D], BF16) for i in range(2)]
        xnfm = [sb(em, ab, f"{L}_xnfm{i}", [128, 8, 128], BF16) for i in range(2)]
        qktm = [sb(em, ab, f"{L}_qktm{i}", [128, 2048], BF16) for i in range(2)]
        cstile = [sb(em, ab, f"{L}_cs{i}", [128, D], BF16) for i in range(2)]
        sntile = [sb(em, ab, f"{L}_sn{i}", [128, D], BF16) for i in range(2)]
        rtmp = sb(em, ab, f"{L}_rtmp", [128, D], BF16)
        pp = [ps(em, ab, f"{L}_pp{i}", [128, 512], F32) for i in range(2)]
        ptr = [ps(em, ab, f"{L}_ptr{i}", [128, 1024], BF16) for i in range(2)]

        for wt, wb_, wkey in ((wqt, wq_b, f"{L}wqb"), (wkt, wk_b, f"{L}wkb"),
                              (wvt, wv_b, f"{L}wvb")):
            for j in range(8):
                em.emit("q", lambda q, wt=wt, wb_=wb_, j=j: q.dma_start(
                    out=wt[:, j, :], in_=wb_[0:D, j * 128:(j + 1) * 128],
                    transpose=True),
                    reads=[wkey], writes=[(wt.name, j)], dma=True)

        for c in range(NC_SEQ):
            b = c % 2
            xk = ("xch", b)
            em.emit("q", lambda q, b=b, c=c: q.dma_start(
                out=xch[b][:, :],
                in_=X_in[row0 + c * 128: row0 + (c + 1) * 128, :]),
                writes=[xk], dma=True)
            if layer == 0:
                em.emit("q", lambda q, b=b, c=c: q.dma_start(
                    out=cstile[b][:, :], in_=rope_cos[c * 128:(c + 1) * 128, :]),
                    writes=[("cs", b)], dma=True)
                em.emit("q", lambda q, b=b, c=c: q.dma_start(
                    out=sntile[b][:, :], in_=rope_sin[c * 128:(c + 1) * 128, :]),
                    writes=[("sn", b)], dma=True)
            emit_ln_chunk(em, lnb, xch[b], xk, xntm[b][:, :], ("xntm", b))
            for j in range(8):
                em.emit("t", lambda t, b=b, j=j: t.transpose(
                    ptr[0][:, j * 128:(j + 1) * 128],
                    xntm[b][:, j * 128:(j + 1) * 128], ident[:, :]),
                    reads=[("xntm", b), "c_ident"], writes=[("ptr", 0)])
            em.emit("v", lambda v, b=b: v.tensor_copy(xnfm[b][:, :, :],
                                                      ptr[0][:, :]),
                    reads=[("ptr", 0)], writes=[("xnfm", b)])
            for pi, (isv, off) in enumerate(((False, 0), (False, 1024),
                                             (True, 0))):
                wt = (wqt, wkt, wvt)[pi]
                for half in range(2):
                    pb = pp[half]
                    for j in range(8):
                        em.emit("t", lambda t, pb=pb, b=b, j=j, wt=wt, half=half:
                                t.matmul(pb[:, :], xnfm[b][:, j, :],
                                         wt[:, j, half * 512:(half + 1) * 512],
                                         start=(j == 0), stop=(j == 7)),
                                reads=[("xnfm", b), (wt.name, j)],
                                writes=[("pp", half)])
                    if isv:
                        em.emit("v", lambda v, pb=pb, c=c, half=half:
                                v.tensor_copy(
                                    vtm[:, c, half * 8:(half + 1) * 8, 0:64],
                                    pb[:, :].rearrange("p (h e) -> p h e",
                                                       e=64)),
                                reads=[("pp", half), "vtm_ones"],
                                writes=[("vtm", c)])
                    else:
                        out_ap = qktm[b][:, off + half * 512:off + (half + 1) * 512]
                        em.emit("v", lambda v, pb=pb, out_ap=out_ap:
                                v.tensor_copy(out_ap, pb[:, :]),
                                reads=[("pp", half)], writes=[("qktm", b)])
            if layer == 0:
                for off in (0, 1024):
                    qk = ("qktm", b)
                    em.emit("v", lambda v, b=b, off=off: v.tensor_tensor(
                        hv(rtmp, 0, True), hv(qktm[b], off, False),
                        hv(sntile[b], 0, True), op=OP.mult),
                        reads=[qk, ("sn", b)], writes=[("rtmp", 0)])
                    em.emit("v", lambda v, b=b, off=off: v.tensor_tensor(
                        hv(rtmp, 0, False), hv(qktm[b], off, True),
                        hv(sntile[b], 0, False), op=OP.mult),
                        reads=[qk, ("sn", b)], writes=[("rtmp", 1)])
                    em.emit("v", lambda v, b=b, off=off: v.tensor_tensor(
                        hv(qktm[b], off, True), hv(qktm[b], off, True),
                        hv(cstile[b], 0, True), op=OP.mult),
                        reads=[qk, ("cs", b)], writes=[qk])
                    em.emit("v", lambda v, b=b, off=off: v.tensor_tensor(
                        hv(qktm[b], off, False), hv(qktm[b], off, False),
                        hv(cstile[b], 0, False), op=OP.mult),
                        reads=[qk, ("cs", b)], writes=[qk])
                    em.emit("v", lambda v, b=b, off=off: v.tensor_tensor(
                        hv(qktm[b], off, True), hv(qktm[b], off, True),
                        hv(rtmp, 0, True), op=OP.subtract),
                        reads=[qk, ("rtmp", 0)], writes=[qk])
                    em.emit("v", lambda v, b=b, off=off: v.tensor_tensor(
                        hv(qktm[b], off, False), hv(qktm[b], off, False),
                        hv(rtmp, 0, False), op=OP.add),
                        reads=[qk, ("rtmp", 1)], writes=[qk])
            for dst, off, pt in ((qfm, 0, 0), (kfm, 1024, 1)):
                for j in range(8):
                    em.emit("t", lambda t, b=b, j=j, off=off, pt=pt: t.transpose(
                        ptr[pt][:, j * 128:(j + 1) * 128],
                        qktm[b][:, off + j * 128:off + (j + 1) * 128],
                        ident[:, :]),
                        reads=[("qktm", b), "c_ident"], writes=[("ptr", pt)])
                em.emit("v", lambda v, dst=dst, c=c, pt=pt: v.tensor_copy(
                    dst[:, :, c * 128:(c + 1) * 128], ptr[pt][:, :]),
                    reads=[("ptr", pt)], writes=[(dst.name, c)])
        em.barrier()
        ab.close()

        ph2 = ExitStack()
        Otm = sb(em, ph2, f"{L}_Otm", [128, NC_SEQ, D], BF16)
        cc = ExitStack()
        pscT = [ps(em, cc, f"{L}_pscT{i}", [128, 512], F32) for i in range(2)]
        pav = [ps(em, cc, f"{L}_pav{i}", [DH + 1, 512], F32) for i in range(2)]
        ptrT = [ps(em, cc, f"{L}_ptrT{i}", [128, DH + 1], BF16)
                for i in range(2)]
        aTe = [sb(em, cc, f"{L}_aTe{i}", [128, 16, 512], BF16)
               for i in range(2)]
        poS = [sb(em, cc, f"{L}_poS{i}", [DH + 1, 512], BF16) for i in range(2)]
        rcpb = [sb(em, cc, f"{L}_rcp{i}", [128, 1], F32) for i in range(2)]

        def post_head(g, h, hb):
            pob = pav[hb]
            poSb = poS[hb]
            em.emit("v", lambda v: v.tensor_copy(poSb[:, :], pob[:, :]),
                    reads=[("pav", hb)], writes=[("poS", hb)])
            for qb in range(4):
                pt = ptrT[qb % 2]
                rc_ = rcpb[qb % 2]
                em.emit("t", lambda t, pt=pt, poSb=poSb, qb=qb: t.transpose(
                    pt[:, :], poSb[:, qb * 128:(qb + 1) * 128],
                    ident[0:DH + 1, 0:DH + 1]),
                    reads=[("poS", hb), "c_ident"], writes=[("ptrT", qb % 2)])
                em.emit("v", lambda v, pt=pt, rc_=rc_: v.reciprocal(
                    rc_[:, :], pt[:, DH:DH + 1]),
                    reads=[("ptrT", qb % 2)], writes=[("rcpb", qb % 2)])
                em.emit("v", lambda v, pt=pt, rc_=rc_, qb=qb, g=g, h=h:
                        v.tensor_scalar(
                            Otm[:, 4 * g + qb, h * 64:(h + 1) * 64],
                            pt[:, 0:DH], rc_[:, :], None, op0=OP.mult),
                        reads=[("ptrT", qb % 2), ("rcpb", qb % 2)],
                        writes=[("Otm", 4 * g + qb)])

        post = None
        for g in range(NQG):
            kcmax = 4 * (g + 1)
            for h in range(H):
                j, pofs = h // 2, (h % 2) * 64
                hb = h % 2
                aT = aTe[hb]
                for kc in range(kcmax):
                    pb = pscT[kc % 2]
                    pkey = ("pscT", kc % 2)
                    is_diag = kc >= 4 * g
                    em.emit("t", lambda t, pb=pb, j=j, pofs=pofs, kc=kc, g=g,
                            is_diag=is_diag: t.matmul(
                                pb[:, :],
                                kfm[pofs:pofs + 64, j, kc * 128:(kc + 1) * 128],
                                qfm[pofs:pofs + 64, j, g * 512:(g + 1) * 512],
                                start=True, stop=not is_diag),
                            reads=[(kfm.name, kc)] +
                                  [(qfm.name, 4 * g + u) for u in range(4)],
                            writes=[pkey])
                    if is_diag:
                        kcl = kc - 4 * g
                        em.emit("t", lambda t, pb=pb, kcl=kcl: t.matmul(
                            pb[:, :], ident[:, :], trimg[:, kcl, :],
                            start=False, stop=True),
                            reads=["c_ident", "c_trimg"], writes=[pkey])
                    em.emit("s", lambda s_, pb=pb, aT=aT, kc=kc: s_.activation(
                        aT[:, kc, :], pb[:, :], AF.Exp, scale=0.125),
                        reads=[pkey], writes=[("aTe", hb, kc)])
                if post is not None:
                    post()
                    post = None
                pob = pav[hb]
                for kc in range(kcmax):
                    em.emit("t", lambda t, pob=pob, kc=kc, h=h, aT=aT, hb=hb,
                            kcmax=kcmax: t.matmul(
                                pob[:, :], vtm[:, kc, h, :], aT[:, kc, :],
                                start=(kc == 0), stop=(kc == kcmax - 1)),
                            reads=[("aTe", hb, kc), ("vtm", kc), "vtm_ones"],
                            writes=[("pav", hb)])
                post = lambda g=g, h=h, hb=hb: post_head(g, h, hb)
        post()
        post = None
        em.barrier()
        cc.close()

        dd = ExitStack()
        wot = sb(em, dd, f"{L}_wot", [128, 8, D], BF16)
        xch2 = [sb(em, dd, f"{L}_x2{i}", [128, D], F32) for i in range(2)]
        xo = [sb(em, dd, f"{L}_xo{i}", [128, D], F32) for i in range(2)]
        otf = [sb(em, dd, f"{L}_otf{i}", [128, 8, 128], BF16) for i in range(2)]
        pd = [ps(em, dd, f"{L}_pd{i}", [128, 512], F32) for i in range(2)]
        ptrD = [ps(em, dd, f"{L}_ptrD{i}", [128, 1024], BF16) for i in range(2)]
        for jj in range(8):
            if layer == 0:
                em.emit("q", lambda q, jj=jj: q.dma_start(
                    out=wot[:, jj, :], in_=wo_b[0:D, jj * 128:(jj + 1) * 128],
                    transpose=True),
                    reads=[f"{L}wob"], writes=[("wot", jj)], dma=True)
            else:
                em.emit("g", lambda g, jj=jj: g.dma_start(
                    out=wot[:, jj, :], in_=Wo[jj * 128:(jj + 1) * 128, 0:D]),
                    writes=[("wot", jj)], dma=True)
        for c in range(NC_SEQ):
            b = c % 2
            x2k = ("xch2", b)
            em.emit("q", lambda q, b=b, c=c: q.dma_start(
                out=xch2[b][:, :],
                in_=X_in[row0 + c * 128: row0 + (c + 1) * 128, :]),
                writes=[x2k], dma=True)
            pD = ptrD[b]
            for j in range(8):
                em.emit("t", lambda t, pD=pD, c=c, j=j: t.transpose(
                    pD[:, j * 128:(j + 1) * 128],
                    Otm[:, c, j * 128:(j + 1) * 128], ident[:, :]),
                    reads=[("Otm", c), "c_ident"], writes=[("ptrD", b)])
            em.emit("v", lambda v, b=b, pD=pD: v.tensor_copy(
                otf[b][:, :, :], pD[:, :]),
                reads=[("ptrD", b)], writes=[("otf", b)])
            for half in range(2):
                pb = pd[half]
                for jj in range(8):
                    em.emit("t", lambda t, pb=pb, b=b, jj=jj, half=half:
                            t.matmul(pb[:, :], otf[b][:, jj, :],
                                     wot[:, jj, half * 512:(half + 1) * 512],
                                     start=(jj == 0), stop=(jj == 7)),
                            reads=[("otf", b), ("wot", jj)],
                            writes=[("pd", half)])
                em.emit("v", lambda v, pb=pb, b=b, half=half: v.tensor_tensor(
                    xo[b][:, half * 512:(half + 1) * 512], pb[:, :],
                    xch2[b][:, half * 512:(half + 1) * 512], op=OP.add),
                    reads=[("pd", half), x2k], writes=[("xo", b)])
            em.emit("q", lambda q, b=b, c=c: q.dma_start(
                out=X_out[row0 + c * 128: row0 + (c + 1) * 128, :],
                in_=xo[b][:, :]),
                reads=[("xo", b)], writes=[("Xout", s, c)], dma=True)
        em.barrier()
        dd.close()
        ph2.close()

    for s in range(NSEQ):
        one_seq(s)
    ph.close()


def emit_router_phase(em, stack, X_in, xnf, Wr, br, g_out, cnt_out, layer):
    nc = em.nc
    L = f"L{layer}r"
    ident, tris = em.consts["ident"], em.consts["tris"]

    rr = ExitStack()
    wr_sb = sb(em, rr, f"{L}_wr", [8, D], BF16)
    wrt = sb(em, rr, f"{L}_wrt", [128, 8, 8], BF16)
    br_sb = sb(em, rr, f"{L}_br", [1, 8], F32)
    ones_f = sb(em, rr, f"{L}_ones", [1, 128], F32)
    xch = [sb(em, rr, f"{L}_xch{i}", [128, D], F32) for i in range(2)]
    xntm = [sb(em, rr, f"{L}_xntm{i}", [128, D], BF16) for i in range(2)]
    xnfm = [sb(em, rr, f"{L}_xnfm{i}", [128, 8, 128], BF16) for i in range(2)]
    lnb = mk_lnb(em, rr, L)
    probs = sb(em, rr, f"{L}_probs", [128, 8], F32)
    m1 = sb(em, rr, f"{L}_m1", [128, 1], F32)
    eq1 = sb(em, rr, f"{L}_eq1", [128, 8], F32)
    pm = sb(em, rr, f"{L}_pm", [128, 8], F32)
    m2 = sb(em, rr, f"{L}_m2", [128, 1], F32)
    keep = sb(em, rr, f"{L}_keep", [128, 8], F32)
    expv = sb(em, rr, f"{L}_expv", [128, 8], F32)
    ssum = sb(em, rr, f"{L}_ssum", [128, 1], F32)
    rcp = sb(em, rr, f"{L}_rcp", [128, 1], F32)
    gsb = sb(em, rr, f"{L}_gsb", [128, NCH, 8], F32)
    msb = sb(em, rr, f"{L}_msb", [128, NCH, 8], F32)
    ptr = ps(em, rr, f"{L}_ptr", [128, 1024], BF16)
    ptw = ps(em, rr, f"{L}_ptw", [128, 64], BF16)
    plg = ps(em, rr, f"{L}_plg", [128, 8], F32)
    pcnt = ps(em, rr, f"{L}_pcnt", [1, NCH * 8], F32)

    em.emit("v", lambda v: v.memset(ones_f[:, :], 1.0), writes=["r_ones"])
    em.emit("g", lambda g: g.dma_start(out=wr_sb[:, :], in_=Wr),
            writes=["r_wr"], dma=True)
    if br is not None:
        em.emit("q", lambda q: q.dma_start(out=br_sb[:, :], in_=br[None, :]),
                writes=["r_br"], dma=True)
    for j in range(8):
        em.emit("t", lambda t, j=j: t.transpose(
            ptw[:, j * 8:(j + 1) * 8], wr_sb[:, j * 128:(j + 1) * 128],
            ident[0:8, 0:8]),
            reads=["r_wr", "c_ident"], writes=[("ptw",)])
    em.emit("v", lambda v: v.tensor_copy(wrt[:, :, :], ptw[:, :]),
            reads=[("ptw",)], writes=["r_wrt"])

    def one_chunk(c):
        b = c % 2
        xk = ("rxch", b)
        em.emit("q", lambda q: q.dma_start(
            out=xch[b][:, :], in_=X_in[c * 128:(c + 1) * 128, :]),
            writes=[xk], dma=True)
        emit_ln_chunk(em, lnb, xch[b], xk, xntm[b][:, :], ("rxntm", b))
        for j in range(8):
            em.emit("t", lambda t, j=j: t.transpose(
                ptr[:, j * 128:(j + 1) * 128],
                xntm[b][:, j * 128:(j + 1) * 128], ident[:, :]),
                reads=[("rxntm", b), "c_ident"], writes=[("rptr",)])
        em.emit("v", lambda v: v.tensor_copy(xnfm[b][:, :, :], ptr[:, :]),
                reads=[("rptr",)], writes=[("rxnfm", b)])
        em.emit("q", lambda q: q.dma_start(
            out=xnf[:, :, c * 128:(c + 1) * 128].rearrange("j p t -> p j t"),
            in_=xnfm[b][:, :, :]),
            reads=[("rxnfm", b)], writes=[("xnf", c)], dma=True)
        for j in range(8):
            em.emit("t", lambda t, j=j: t.matmul(
                plg[:, :], xnfm[b][:, j, :], wrt[:, j, :],
                start=(j == 0), stop=(br is None and j == 7)),
                reads=[("rxnfm", b), "r_wrt"], writes=[("plg",)])
        if br is not None:
            em.emit("t", lambda t: t.matmul(
                plg[:, :], ones_f[:, :], br_sb[:, :], start=False, stop=True),
                reads=["r_ones", "r_br"], writes=[("plg",)])
        em.emit("v", lambda v: v.reduce_max(m1[:, :], plg[:, :], axis=AX.X,
                                            negate=True),
                reads=[("plg",)], writes=["r_m1"])
        em.emit("s", lambda s_: s_.activation(expv[:, :], plg[:, :], AF.Exp,
                                              bias=m1[:, :],
                                              accum_out=ssum[:, :]),
                reads=[("plg",), "r_m1"], writes=["r_expv", "r_ssum"])
        em.emit("v", lambda v: v.reciprocal(rcp[:, :], ssum[:, :]),
                reads=["r_ssum"], writes=["r_rcp"])
        em.emit("v", lambda v: v.tensor_scalar(probs[:, :], expv[:, :],
                                               rcp[:, :], None, op0=OP.mult),
                reads=["r_expv", "r_rcp"], writes=["r_probs"])
        em.emit("v", lambda v: v.reduce_max(m1[:, :], probs[:, :], axis=AX.X),
                reads=["r_probs"], writes=["r_m1"])
        em.emit("v", lambda v: v.tensor_scalar(eq1[:, :], probs[:, :],
                                               m1[:, :], None, op0=OP.is_ge),
                reads=["r_probs", "r_m1"], writes=["r_eq1"])
        em.emit("v", lambda v: v.scalar_tensor_tensor(
            pm[:, :], eq1[:, :], NEG, probs[:, :], op0=OP.mult, op1=OP.add),
            reads=["r_eq1", "r_probs"], writes=["r_pm"])
        em.emit("v", lambda v: v.reduce_max(m2[:, :], pm[:, :], axis=AX.X),
                reads=["r_pm"], writes=["r_m2"])
        em.emit("v", lambda v: v.tensor_scalar(keep[:, :], probs[:, :],
                                               m2[:, :], None, op0=OP.is_ge),
                reads=["r_probs", "r_m2"], writes=["r_keep"])
        if layer == 0:
            em.emit("v", lambda v, c=c: v.tensor_tensor(
                gsb[:, c, :], probs[:, :], keep[:, :], op=OP.mult),
                reads=["r_probs", "r_keep"], writes=[("gsb", c)])
            em.emit("v", lambda v, c=c: v.tensor_copy(msb[:, c, :], keep[:, :]),
                    reads=["r_keep"], writes=[("msb", c)])
        else:
            em.emit("v", lambda v, c=c: v.tensor_tensor(
                msb[:, c, :], probs[:, :], keep[:, :], op=OP.mult),
                reads=["r_probs", "r_keep"], writes=[("msb", c)])
            em.emit("v", lambda v, c=c: v.tensor_reduce(
                ssum[:, :], msb[:, c, :], axis=AX.X, op=OP.add),
                reads=[("msb", c)], writes=["r_ssum"])
            em.emit("v", lambda v: v.reciprocal(rcp[:, :], ssum[:, :]),
                    reads=["r_ssum"], writes=["r_rcp"])
            em.emit("v", lambda v, c=c: v.tensor_scalar(
                gsb[:, c, :], msb[:, c, :], rcp[:, :], None, op0=OP.mult),
                reads=[("msb", c), "r_rcp"], writes=[("gsb", c)])

    for c in range(NCH):
        one_chunk(c)
    em.emit("q", lambda q: q.dma_start(
        out=g_out, in_=gsb[:, :, :]),
        reads=[("gsb", c) for c in range(NCH)], writes=["g_out"], dma=True)
    if layer == 0:
        tot_sb = sb(em, rr, f"{L}_tot", [1, NCH * 8], F32)
        cnt_sb = sb(em, rr, f"{L}_cnt", [1, 8], F32)
        ones_col = sb(em, rr, f"{L}_onesc", [128, 1], F32)
        em.emit("v", lambda v: v.memset(ones_col[:, :], 1.0), writes=["r_onc"])
        em.emit("t", lambda t: t.matmul(
            pcnt[:, :], ones_col[:, :], msb[:, :, :], start=True, stop=True),
            reads=["r_onc"] + [("msb", c) for c in range(NCH)],
            writes=[("pcnt",)])
        em.emit("v", lambda v: v.tensor_copy(tot_sb[:, :], pcnt[:, :]),
                reads=[("pcnt",)], writes=["r_tot"])
        em.emit("v", lambda v: v.tensor_reduce(
            cnt_sb[:, :], tot_sb[:, :].rearrange("p (c e) -> p e c", c=NCH),
            axis=AX.X, op=OP.add),
            reads=["r_tot"], writes=["r_cnt"])
        em.emit("q", lambda q: q.dma_start(out=cnt_out, in_=cnt_sb[:, :]),
                reads=["r_cnt"], writes=["cnt_out"], dma=True)
    em.barrier()
    rr.close()


def emit_kept(em, stack, g_in, allcnt, prefmask, gk_sb):
    nc = em.nc
    tris = em.consts["tris"]
    kk = ExitStack()
    g0s = gk_sb
    W = NCH * 8
    msk = sb(em, kk, "k_msk", [128, W], F32)
    allc = sb(em, kk, "k_allc", [8, 8], F32)
    pref = sb(em, kk, "k_pref", [8, 1], F32)
    onesc = sb(em, kk, "k_onesc", [128, 1], F32)
    onesr = sb(em, kk, "k_onesr", [1, 128], F32)
    tot = sb(em, kk, "k_tot", [1, W], F32)
    totc = sb(em, kk, "k_totc", [NCH, 8], F32)
    coff = sb(em, kk, "k_coff", [1, 8], F32)
    ecc = sb(em, kk, "k_ecc", [NCH, 8], F32)
    eccrow = sb(em, kk, "k_eccrow", [1, W], F32)
    kept = sb(em, kk, "k_kept", [128, W], F32)
    pex = ps(em, kk, "k_pex", [128, W], F32)
    paux = ps(em, kk, "k_paux", [max(NCH, 64), 512], F32)
    em.emit("q", lambda q: q.dma_start(out=g0s[:, :, :], in_=g_in),
            writes=["k_g0"], dma=True)
    em.emit("q", lambda q: q.dma_start(out=allc[:, :], in_=allcnt),
            writes=["k_allc"], dma=True)
    em.emit("q", lambda q: q.dma_start(out=pref[:, :], in_=prefmask),
            writes=["k_pref"], dma=True)
    em.emit("v", lambda v: v.memset(onesc[:, :], 1.0), writes=["k_onc"])
    em.emit("v", lambda v: v.memset(onesr[:, :], 1.0), writes=["k_onr"])
    em.emit("v", lambda v: v.tensor_scalar(
        msk[:, :], g0s[:, :, :], 0.0, None, op0=OP.is_gt),
        reads=["k_g0"], writes=["k_msk"])
    em.emit("t", lambda t: t.matmul(pex[:, :], tris[:, :], msk[:, :],
                                    start=True, stop=False),
            reads=["c_tris", "k_msk"], writes=[("k_pex",)])
    em.emit("t", lambda t: t.matmul(paux[0:1, 0:W], onesc[:, :], msk[:, :],
                                    start=True, stop=True),
            reads=["k_onc", "k_msk"], writes=[("k_paux",)])
    em.emit("v", lambda v: v.tensor_copy(tot[:, :], paux[0:1, 0:W]),
            reads=[("k_paux",)], writes=["k_tot"])
    em.emit("g", lambda g: g.dma_start(
        out=totc[:, :], in_=tot[:, :].rearrange("p (c e) -> p c e", c=NCH)),
        reads=["k_tot"], writes=["k_totc"], dma=True)
    em.emit("t", lambda t: t.matmul(paux[0:1, 0:8], pref[:, :], allc[:, :],
                                    start=True, stop=True),
            reads=["k_pref", "k_allc"], writes=[("k_paux",)])
    em.emit("v", lambda v: v.tensor_copy(coff[:, :], paux[0:1, 0:8]),
            reads=[("k_paux",)], writes=["k_coff"])
    em.emit("t", lambda t: t.matmul(paux[0:NCH, 0:8], tris[0:NCH, 0:NCH],
                                    totc[:, :], start=True, stop=False),
            reads=["c_tris", "k_totc"], writes=[("k_paux",)])
    em.emit("t", lambda t: t.matmul(paux[0:NCH, 0:8], onesr[:, 0:NCH],
                                    coff[:, :], start=False, stop=True),
            reads=["k_onr", "k_coff"], writes=[("k_paux",)])
    em.emit("v", lambda v: v.tensor_copy(ecc[:, :], paux[0:NCH, 0:8]),
            reads=[("k_paux",)], writes=["k_ecc"])
    em.emit("g", lambda g: g.dma_start(
        out=eccrow[:, :].rearrange("p (c e) -> p c e", c=NCH), in_=ecc[:, :]),
        reads=["k_ecc"], writes=["k_eccrow"], dma=True)
    em.emit("t", lambda t: t.matmul(pex[:, :], onesr[:, :], eccrow[:, :],
                                    start=False, stop=True),
            reads=["k_onr", "k_eccrow"], writes=[("k_pex",)])
    em.emit("v", lambda v: v.tensor_scalar(kept[:, :], pex[:, :], CAP, None,
                                           op0=OP.is_lt),
            reads=[("k_pex",)], writes=["k_kept"])
    em.emit("v", lambda v: v.tensor_tensor(
        g0s[:, :, :], g0s[:, :, :],
        kept[:, :].rearrange("p (c e) -> p c e", c=NCH), op=OP.mult),
        reads=["k_g0", "k_kept"], writes=["gk"])
    em.barrier()
    kk.close()


def emit_experts0_phase(em, stack, X_in, X_out, xnf, g_in, allcnt, prefmask,
                        Wi_all, Wo_all):
    nc = em.nc
    L = "L0e"
    ee = ExitStack()
    gk = sb(em, ee, f"{L}_gk", [128, NCH, 8], F32)
    emit_kept(em, ee, g_in, allcnt, prefmask, gk)

    for half in range(4):
        r0, r1 = half * (NTOK // 4), (half + 1) * (NTOK // 4)
        em.emit("q", lambda q, r0=r0, r1=r1: q.dma_start(
            out=X_out[r0:r1, :], in_=X_in[r0:r1, :]),
            writes=[("Xo_init", half)], dma=True)

    wi_sb = sb(em, ee, f"{L}_wi", [128, 8, 4096], BF16)
    wo_sb = sb(em, ee, f"{L}_wo", [128, 32, D], BF16)
    xt = [sb(em, ee, f"{L}_xt{i}", [128, 8, 512], BF16) for i in range(2)]
    hm = sb(em, ee, f"{L}_hm", [128, 32, 512], BF16)
    yt = [sb(em, ee, f"{L}_yt{i}", [128, D], F32) for i in range(2)]
    php = [ps(em, ee, f"{L}_php{i}", [128, 512], F32) for i in range(2)]
    pyp = [ps(em, ee, f"{L}_pyp{i}", [128, 512], F32) for i in range(2)]
    NT = NTOK // 512

    def one_expert(e):
        for j in range(8):
            em.emit("g", lambda g, j=j: g.dma_start(
                out=wi_sb[:, j, :], in_=Wi_all[e, j * 128:(j + 1) * 128, :]),
                writes=[("wi", j)], dma=True)
        for i4 in range(4):
            em.emit("g", lambda g, i4=i4: g.dma_start(
                out=wo_sb[:, i4 * 8:(i4 + 1) * 8, :],
                in_=Wo_all[e, i4 * 1024:(i4 + 1) * 1024, :].rearrange(
                    "(i p) h -> p i h", p=128)),
                writes=[("wo", i4)], dma=True)

        def one_ttile(tt):
            bx = tt % 2
            t0 = tt * 512
            em.emit("q", lambda q: q.dma_start(
                out=xt[bx][:, :, :],
                in_=xnf[:, :, t0:t0 + 512].rearrange("j p t -> p j t")),
                writes=[("xt", bx)], dma=True)
            for ic in range(32):
                pb = php[ic % 2]
                for j in range(8):
                    em.emit("t", lambda t, ic=ic, j=j, pb=pb: t.matmul(
                        pb[:, :], wi_sb[:, j, ic * 128:(ic + 1) * 128],
                        xt[bx][:, j, :], start=(j == 0), stop=(j == 7)),
                        reads=[("wi", j), ("xt", bx)], writes=[("php", ic % 2)])
                eng = "s" if ic % 2 == 0 else "v"
                if eng == "s":
                    em.emit("s", lambda s_, ic=ic, pb=pb: s_.activation(
                        hm[:, ic, :], pb[:, :], AF.Relu),
                        reads=[("php", ic % 2)], writes=[("hm", ic)])
                else:
                    em.emit("v", lambda v, ic=ic, pb=pb: v.tensor_scalar(
                        hm[:, ic, :], pb[:, :], 0.0, None, op0=OP.max),
                        reads=[("php", ic % 2)], writes=[("hm", ic)])
            for tsub in range(4):
                c = tt * 4 + tsub
                by = tsub % 2
                for hh in range(2):
                    pb = pyp[by]
                    for ic in range(32):
                        em.emit("t", lambda t, ic=ic, tsub=tsub, hh=hh, pb=pb:
                                t.matmul(pb[:, :],
                                         hm[:, ic, tsub * 128:(tsub + 1) * 128],
                                         wo_sb[:, ic, hh * 512:(hh + 1) * 512],
                                         start=(ic == 0), stop=(ic == 31)),
                                reads=[("hm", ic), ("wo", ic // 8)],
                                writes=[("pyp", by)])
                    em.emit("v", lambda v, pb=pb, by=by, hh=hh, c=c: v.tensor_scalar(
                        yt[by][:, hh * 512:(hh + 1) * 512], pb[:, :],
                        gk[:, c, e:e + 1], None, op0=OP.mult),
                        reads=[("pyp", by), "gk"], writes=[("yt", by)])
                em.emit("g", lambda g, by=by, c=c: g.dma_start(
                    out=X_out[c * 128:(c + 1) * 128, :], in_=yt[by][:, :],
                    accum_op=OP.add, max_dma_last_dim=4096),
                    reads=[("yt", by), ("Xo_init", (c * 128) // (NTOK // 4))],
                    writes=[("Xo", c)], dma=True)

        for tt in range(NT):
            one_ttile(tt)

    for e in range(E):
        one_expert(e)
    em.barrier()
    ee.close()


def emit_moe1_phase(em, stack, X_in, X_out, xnf, g_in, gup_all, dn_all):
    nc = em.nc
    L = "L1e"
    gup_b = nc.dram_tensor(f"{L}_gup_b", [E, 8192, D], BF16).ap()
    dn_b = nc.dram_tensor(f"{L}_dn_b", [E, D, 4096], BF16).ap()
    for e in range(E):
        emit_cast(em, gup_b[e, :, :], gup_all[e, :, :], (f"{L}gupb", e))
        emit_cast(em, dn_b[e, :, :], dn_all[e, :, :], (f"{L}dnb", e))

    ee = ExitStack()
    g1 = sb(em, ee, f"{L}_g1", [128, NCH, 8], F32)
    em.emit("q", lambda q: q.dma_start(out=g1[:, :, :], in_=g_in),
            writes=["g1"], dma=True)

    for half in range(4):
        r0, r1 = half * (NTOK // 4), (half + 1) * (NTOK // 4)
        em.emit("q", lambda q, r0=r0, r1=r1: q.dma_start(
            out=X_out[r0:r1, :], in_=X_in[r0:r1, :]),
            writes=[("Xo_init", half)], dma=True)

    gt = sb(em, ee, f"{L}_gt", [128, 8, 2048], BF16)
    ut = sb(em, ee, f"{L}_ut", [128, 8, 2048], BF16)
    dt = sb(em, ee, f"{L}_dt", [128, 16, D], BF16)
    xt = [sb(em, ee, f"{L}_xt{i}", [128, 8, 512], BF16) for i in range(2)]
    ha = sb(em, ee, f"{L}_ha", [128, 16, 512], BF16)
    sg = sb(em, ee, f"{L}_sg", [128, 512], F32)
    yt = [sb(em, ee, f"{L}_yt{i}", [128, D], F32) for i in range(2)]
    pg = ps(em, ee, f"{L}_pg", [128, 512], F32)
    pu = ps(em, ee, f"{L}_pu", [128, 512], F32)
    pyp = [ps(em, ee, f"{L}_pyp{i}", [128, 512], F32) for i in range(2)]
    NT = NTOK // 512

    def one_half(e, ih):
        for j in range(8):
            em.emit("q", lambda q, j=j, e=e, ih=ih: q.dma_start(
                out=gt[:, j, :],
                in_=gup_b[e, ih * 2048:(ih + 1) * 2048, j * 128:(j + 1) * 128],
                transpose=True),
                reads=[(f"{L}gupb", e)], writes=[("gt", j)], dma=True)
            em.emit("q", lambda q, j=j, e=e, ih=ih: q.dma_start(
                out=ut[:, j, :],
                in_=gup_b[e, 4096 + ih * 2048:4096 + (ih + 1) * 2048,
                          j * 128:(j + 1) * 128],
                transpose=True),
                reads=[(f"{L}gupb", e)], writes=[("ut", j)], dma=True)
        for ic in range(16):
            em.emit("q", lambda q, ic=ic, e=e, ih=ih: q.dma_start(
                out=dt[:, ic, :],
                in_=dn_b[e, :,
                         ih * 2048 + ic * 128: ih * 2048 + (ic + 1) * 128],
                transpose=True),
                reads=[(f"{L}dnb", e)], writes=[("dt", ic)], dma=True)

        def one_ttile(tt):
            bx = tt % 2
            t0 = tt * 512
            em.emit("q", lambda q: q.dma_start(
                out=xt[bx][:, :, :],
                in_=xnf[:, :, t0:t0 + 512].rearrange("j p t -> p j t")),
                writes=[("xt", bx)], dma=True)
            for oc in range(16):
                for j in range(8):
                    em.emit("t", lambda t, oc=oc, j=j: t.matmul(
                        pg[:, :], gt[:, j, oc * 128:(oc + 1) * 128],
                        xt[bx][:, j, :], start=(j == 0), stop=(j == 7)),
                        reads=[("gt", j), ("xt", bx)], writes=[("pg",)])
                for j in range(8):
                    em.emit("t", lambda t, oc=oc, j=j: t.matmul(
                        pu[:, :], ut[:, j, oc * 128:(oc + 1) * 128],
                        xt[bx][:, j, :], start=(j == 0), stop=(j == 7)),
                        reads=[("ut", j), ("xt", bx)], writes=[("pu",)])
                em.emit("s", lambda s_, oc=oc: s_.activation(
                    sg[:, :], pg[:, :], AF.Silu),
                    reads=[("pg",)], writes=["sg"])
                em.emit("v", lambda v, oc=oc: v.tensor_tensor(
                    ha[:, oc, :], sg[:, :], pu[:, :], op=OP.mult),
                    reads=["sg", ("pu",)], writes=[("ha", oc)])
            for tsub in range(4):
                c = tt * 4 + tsub
                by = tsub % 2
                for hh in range(2):
                    pb = pyp[by]
                    for ic in range(16):
                        em.emit("t", lambda t, ic=ic, tsub=tsub, hh=hh, pb=pb:
                                t.matmul(pb[:, :],
                                         ha[:, ic, tsub * 128:(tsub + 1) * 128],
                                         dt[:, ic, hh * 512:(hh + 1) * 512],
                                         start=(ic == 0), stop=(ic == 15)),
                                reads=[("ha", ic), ("dt", ic)],
                                writes=[("pyp", by)])
                    em.emit("v", lambda v, pb=pb, by=by, hh=hh, c=c:
                            v.tensor_scalar(
                                yt[by][:, hh * 512:(hh + 1) * 512], pb[:, :],
                                g1[:, c, e:e + 1], None, op0=OP.mult),
                            reads=[("pyp", by), "g1"], writes=[("yt", by)])
                em.emit("g", lambda g, by=by, c=c: g.dma_start(
                    out=X_out[c * 128:(c + 1) * 128, :], in_=yt[by][:, :],
                    accum_op=OP.add, max_dma_last_dim=4096),
                    reads=[("yt", by), ("Xo_init", (c * 128) // (NTOK // 4))],
                    writes=[("Xo", c)], dma=True)

        for tt in range(NT):
            one_ttile(tt)

    for e in range(E):
        for ih in range(2):
            one_half(e, ih)
    em.barrier()
    ee.close()


def build_attn_program(layer):
    nc = bass.Bass("TRN2", target_bir_lowering=False)
    X_in = nc.dram_tensor("X", [NTOK, D], F32, kind="ExternalInput").ap()
    Wq = nc.dram_tensor("Wq", [D, D], F32, kind="ExternalInput").ap()
    Wk = nc.dram_tensor("Wk", [D, D], F32, kind="ExternalInput").ap()
    Wv = nc.dram_tensor("Wv", [D, D], F32, kind="ExternalInput").ap()
    Wo = nc.dram_tensor("Wo", [D, D], F32, kind="ExternalInput").ap()
    if layer == 0:
        rc = nc.dram_tensor("rope_cos", [S, D], BF16, kind="ExternalInput").ap()
        rs_ = nc.dram_tensor("rope_sin", [S, D], BF16, kind="ExternalInput").ap()
    else:
        rc = rs_ = None
    X_out = nc.dram_tensor("Xo", [NTOK, D], F32, kind="ExternalOutput").ap()
    em = Em(nc)
    cstack = ExitStack()
    emit_consts(em, cstack)
    emit_attn_phase(em, cstack, X_in, X_out, Wq, Wk, Wv, Wo, rc, rs_, layer)
    em.finalize()
    cstack.close()
    return nc


def build_router_program(layer):
    nc = bass.Bass("TRN2", target_bir_lowering=False)
    X_in = nc.dram_tensor("X", [NTOK, D], F32, kind="ExternalInput").ap()
    Wr = nc.dram_tensor("Wr", [E, D], F32, kind="ExternalInput").ap()
    br = (nc.dram_tensor("br", [E], F32, kind="ExternalInput").ap()
          if layer == 0 else None)
    xnf = nc.dram_tensor("xnf", [8, 128, NTOK], BF16, kind="ExternalOutput").ap()
    g_out = nc.dram_tensor("g", [128, NCH * 8], F32, kind="ExternalOutput").ap()
    cnt = (nc.dram_tensor("cnt", [1, E], F32, kind="ExternalOutput").ap()
           if layer == 0 else None)
    em = Em(nc)
    cstack = ExitStack()
    emit_consts(em, cstack)
    emit_router_phase(em, cstack, X_in, xnf, Wr, br, g_out, cnt, layer)
    em.finalize()
    cstack.close()
    return nc


def build_experts0_program():
    nc = bass.Bass("TRN2", target_bir_lowering=False)
    X_in = nc.dram_tensor("X", [NTOK, D], F32, kind="ExternalInput").ap()
    xnf = nc.dram_tensor("xnf", [8, 128, NTOK], BF16, kind="ExternalInput").ap()
    g_in = nc.dram_tensor("g", [128, NCH * 8], F32, kind="ExternalInput").ap()
    allcnt = nc.dram_tensor("allcnt", [NCORES, E], F32, kind="ExternalInput").ap()
    prefmask = nc.dram_tensor("prefmask", [NCORES, 1], F32, kind="ExternalInput").ap()
    Wi = nc.dram_tensor("Wi", [E, D, 4096], F32, kind="ExternalInput").ap()
    Wo = nc.dram_tensor("Wo", [E, 4096, D], F32, kind="ExternalInput").ap()
    X_out = nc.dram_tensor("Xo", [NTOK, D], F32, kind="ExternalOutput").ap()
    em = Em(nc)
    cstack = ExitStack()
    emit_consts(em, cstack)
    emit_experts0_phase(em, cstack, X_in, X_out, xnf, g_in.rearrange(
        "p (c e) -> p c e", e=E), allcnt, prefmask, Wi, Wo)
    em.finalize()
    cstack.close()
    return nc


def build_moe1_program():
    nc = bass.Bass("TRN2", target_bir_lowering=False)
    X_in = nc.dram_tensor("X", [NTOK, D], F32, kind="ExternalInput").ap()
    Wr = nc.dram_tensor("Wr", [E, D], F32, kind="ExternalInput").ap()
    gup = nc.dram_tensor("gup", [E, 8192, D], F32, kind="ExternalInput").ap()
    dn = nc.dram_tensor("dn", [E, D, 4096], F32, kind="ExternalInput").ap()
    X_out = nc.dram_tensor("Xo", [NTOK, D], F32, kind="ExternalOutput").ap()
    xnf = nc.dram_tensor("xnf1", [8, 128, NTOK], BF16).ap()
    g_sc = nc.dram_tensor("g1_sc", [128, NCH * 8], F32).ap()
    em = Em(nc)
    cstack = ExitStack()
    emit_consts(em, cstack)
    emit_router_phase(em, cstack, X_in, xnf, Wr, None, g_sc, None, 1)
    emit_moe1_phase(em, cstack, X_in, X_out, xnf, g_sc, gup, dn)
    em.finalize()
    cstack.close()
    return nc


def _rope_tables():
    inv = 1.0 / (10000.0 ** (np.arange(0, DH, 2, dtype=np.float64) / DH))
    t = np.arange(S, dtype=np.float64)
    freqs = np.outer(t, inv)
    emb = np.concatenate([freqs, freqs], axis=1)
    cos = np.cos(emb).astype(np.float32)
    sin = np.sin(emb).astype(np.float32)
    return (np.tile(cos, (1, H)).astype(ml_dtypes.bfloat16).copy(),
            np.tile(sin, (1, H)).astype(ml_dtypes.bfloat16).copy())


_PROG_CACHE = {}


def _get_prog(key, builder):
    if key not in _PROG_CACHE:
        _PROG_CACHE[key] = builder()
    return _PROG_CACHE[key]



def build_fused_program():
    nc = bass.Bass("TRN2", target_bir_lowering=False)
    x_in = nc.dram_tensor("x", [NTOK, D], F32, kind="ExternalInput").ap()
    Wq0 = nc.dram_tensor("Wq0", [D, D], F32, kind="ExternalInput").ap()
    Wk0 = nc.dram_tensor("Wk0", [D, D], F32, kind="ExternalInput").ap()
    Wv0 = nc.dram_tensor("Wv0", [D, D], F32, kind="ExternalInput").ap()
    Wo0 = nc.dram_tensor("Wo0", [D, D], F32, kind="ExternalInput").ap()
    rc = nc.dram_tensor("rope_cos", [S, D], BF16, kind="ExternalInput").ap()
    rs_ = nc.dram_tensor("rope_sin", [S, D], BF16, kind="ExternalInput").ap()
    Wr0 = nc.dram_tensor("Wr0", [E, D], F32, kind="ExternalInput").ap()
    br0 = nc.dram_tensor("br0", [E], F32, kind="ExternalInput").ap()
    Wi0 = nc.dram_tensor("Wi0", [E, D, 4096], F32, kind="ExternalInput").ap()
    Wo0e = nc.dram_tensor("Wo0e", [E, 4096, D], F32, kind="ExternalInput").ap()
    Wq1 = nc.dram_tensor("Wq1", [D, D], F32, kind="ExternalInput").ap()
    Wk1 = nc.dram_tensor("Wk1", [D, D], F32, kind="ExternalInput").ap()
    Wv1 = nc.dram_tensor("Wv1", [D, D], F32, kind="ExternalInput").ap()
    Wo1 = nc.dram_tensor("Wo1", [D, D], F32, kind="ExternalInput").ap()
    Wr1 = nc.dram_tensor("Wr1", [E, D], F32, kind="ExternalInput").ap()
    gup = nc.dram_tensor("gup", [E, 8192, D], F32, kind="ExternalInput").ap()
    dn = nc.dram_tensor("dn", [E, D, 4096], F32, kind="ExternalInput").ap()
    onehot = nc.dram_tensor("onehot", [1, NCORES], F32, kind="ExternalInput").ap()
    prefmask = nc.dram_tensor("prefmask", [NCORES, 1], F32, kind="ExternalInput").ap()
    out = nc.dram_tensor("out", [NTOK, D], F32, kind="ExternalOutput").ap()

    dbg = "ExternalOutput" if DEBUG_X else "Internal"
    X1 = nc.dram_tensor("X1", [NTOK, D], F32, kind=dbg).ap()
    X2 = nc.dram_tensor("X2", [NTOK, D], F32, kind=dbg).ap()
    X3 = nc.dram_tensor("X3", [NTOK, D], F32, kind=dbg).ap()
    xnf0 = nc.dram_tensor("xnf0", [8, 128, NTOK], BF16).ap()
    xnf1 = nc.dram_tensor("xnf1", [8, 128, NTOK], BF16).ap()
    g0 = nc.dram_tensor("g0_sc", [128, NCH * 8], F32).ap()
    g1 = nc.dram_tensor("g1_sc", [128, NCH * 8], F32).ap()
    cnt_loc = nc.dram_tensor("cnt_loc", [1, E], F32).ap()
    cc_in_t = nc.dram_tensor("cc_in", [NCORES, E], F32)
    cc_out_t = nc.dram_tensor("cc_out", [NCORES, E], F32)

    em = Em(nc)
    cstack = ExitStack()
    emit_consts(em, cstack)

    emit_attn_phase(em, cstack, x_in, X1, Wq0, Wk0, Wv0, Wo0, rc, rs_, 0)
    emit_router_phase(em, cstack, X1, xnf0, Wr0, br0, g0, cnt_loc, 0)

    ex = ExitStack()
    oh_sb = sb(em, ex, "cc_oh", [1, NCORES], F32)
    cl_sb = sb(em, ex, "cc_cl", [1, E], F32)
    ci_sb = sb(em, ex, "cc_ci", [NCORES, E], F32)
    pcc = ps(em, ex, "cc_p", [NCORES, E], F32)
    em.emit("q", lambda q: q.dma_start(out=oh_sb[:, :], in_=onehot),
            writes=["cc_oh"], dma=True)
    em.emit("q", lambda q: q.dma_start(out=cl_sb[:, :], in_=cnt_loc),
            reads=["cnt_out"], writes=["cc_cl"], dma=True)
    em.emit("t", lambda t: t.matmul(pcc[:, :], oh_sb[:, :], cl_sb[:, :],
                                    start=True, stop=True),
            reads=["cc_oh", "cc_cl"], writes=[("cc_p",)])
    em.emit("v", lambda v: v.tensor_copy(ci_sb[:, :], pcc[:, :]),
            reads=[("cc_p",)], writes=["cc_ci"])
    em.emit("q", lambda q: q.dma_start(out=cc_in_t.ap(), in_=ci_sb[:, :]),
            reads=["cc_ci"], writes=["cc_in_d"], dma=True)
    em.barrier(engines=("g",))
    if SIM_CC_COPY:
        em.emit("g", lambda g: g.dma_start(out=cc_out_t.ap(), in_=cc_in_t.ap()),
                reads=["cc_in_d"], writes=["cc_out_d"], dma=True)
    else:
        em.emit("g", lambda g: g.collective_compute(
            "AllReduce", OP.add, replica_groups=[list(range(NCORES))],
            ins=[cc_in_t.ap().opt()], outs=[cc_out_t.ap().opt()]),
            reads=["cc_in_d"], writes=["cc_out_d"])
    em.barrier()
    ex.close()

    emit_experts0_phase(em, cstack, X1, X2, xnf0, g0.rearrange(
        "p (c e) -> p c e", e=E), cc_out_t.ap(), prefmask, Wi0, Wo0e)
    emit_attn_phase(em, cstack, X2, X3, Wq1, Wk1, Wv1, Wo1, None, None, 1)
    emit_router_phase(em, cstack, X3, xnf1, Wr1, None, g1, None, 1)
    emit_moe1_phase(em, cstack, X3, out, xnf1, g1, gup, dn)
    em.finalize()
    cstack.close()
    return nc


def kernel(**inputs):
    xs = np.ascontiguousarray(np.asarray(inputs["L0_attn_x"], dtype=np.float32)
                              .reshape(NCORES, NTOK, D))
    cos, sin = _rope_tables()
    cores = list(range(NCORES))
    wq1 = np.ascontiguousarray(np.asarray(inputs["L1_attn_w_q"],
                                          dtype=np.float32).reshape(D, D))
    wk1 = np.ascontiguousarray(np.asarray(inputs["L1_attn_w_k"],
                                          dtype=np.float32).reshape(D, D))
    wv1 = np.ascontiguousarray(np.asarray(inputs["L1_attn_w_v"],
                                          dtype=np.float32).reshape(D, D))
    nc = _get_prog("fused", build_fused_program)
    im = []
    for c in cores:
        oh = np.zeros((1, NCORES), np.float32)
        oh[0, c] = 1.0
        pref = np.zeros((NCORES, 1), np.float32)
        pref[:c, 0] = 1.0
        im.append(dict(
            x=xs[c], Wq0=inputs["L0_attn_Wq"], Wk0=inputs["L0_attn_Wk"],
            Wv0=inputs["L0_attn_Wv"], Wo0=inputs["L0_attn_Wo"],
            rope_cos=cos, rope_sin=sin,
            Wr0=inputs["L0_ffn_W_router"], br0=inputs["L0_ffn_b_router"],
            Wi0=inputs["L0_ffn_experts_inter"], Wo0e=inputs["L0_ffn_experts_out"],
            Wq1=wq1, Wk1=wk1, Wv1=wv1, Wo1=inputs["L1_attn_w_o"],
            Wr1=inputs["L1_ffn_gate_w"], gup=inputs["L1_ffn_gate_up_proj"],
            dn=inputs["L1_ffn_down_proj"], onehot=oh, prefmask=pref))
    r = run_bass_kernel_spmd(nc, im, core_ids=cores)
    return np.stack([r.results[c]["out"] for c in cores]).reshape(B, S, D)


def kernel_multi(**inputs):
    xs = np.ascontiguousarray(np.asarray(inputs["L0_attn_x"], dtype=np.float32)
                              .reshape(NCORES, NTOK, D))
    cos, sin = _rope_tables()
    cores = list(range(NCORES))

    nc1 = _get_prog("attn0", lambda: build_attn_program(0))
    im = [dict(X=xs[c], Wq=inputs["L0_attn_Wq"], Wk=inputs["L0_attn_Wk"],
               Wv=inputs["L0_attn_Wv"], Wo=inputs["L0_attn_Wo"],
               rope_cos=cos, rope_sin=sin) for c in cores]
    r = run_bass_kernel_spmd(nc1, im, core_ids=cores)
    X = [r.results[c]["Xo"] for c in cores]

    nc2 = _get_prog("router0", lambda: build_router_program(0))
    im = [dict(X=X[c], Wr=inputs["L0_ffn_W_router"],
               br=inputs["L0_ffn_b_router"]) for c in cores]
    r = run_bass_kernel_spmd(nc2, im, core_ids=cores)
    xnf = [r.results[c]["xnf"] for c in cores]
    g0 = [r.results[c]["g"] for c in cores]
    allcnt = np.concatenate([r.results[c]["cnt"] for c in cores], axis=0)
    allcnt = np.ascontiguousarray(allcnt.astype(np.float32))

    nc3 = _get_prog("experts0", build_experts0_program)
    pref = np.zeros((NCORES, NCORES, 1), np.float32)
    for c in cores:
        pref[c, :c, 0] = 1.0
    im = [dict(X=X[c], xnf=xnf[c], g=g0[c], allcnt=allcnt,
               prefmask=pref[c], Wi=inputs["L0_ffn_experts_inter"],
               Wo=inputs["L0_ffn_experts_out"]) for c in cores]
    r = run_bass_kernel_spmd(nc3, im, core_ids=cores)
    X = [r.results[c]["Xo"] for c in cores]

    nc4 = _get_prog("attn1", lambda: build_attn_program(1))
    wq1 = np.ascontiguousarray(np.asarray(inputs["L1_attn_w_q"],
                                          dtype=np.float32).reshape(D, D))
    wk1 = np.ascontiguousarray(np.asarray(inputs["L1_attn_w_k"],
                                          dtype=np.float32).reshape(D, D))
    wv1 = np.ascontiguousarray(np.asarray(inputs["L1_attn_w_v"],
                                          dtype=np.float32).reshape(D, D))
    im = [dict(X=X[c], Wq=wq1, Wk=wk1, Wv=wv1,
               Wo=inputs["L1_attn_w_o"]) for c in cores]
    r = run_bass_kernel_spmd(nc4, im, core_ids=cores)
    X = [r.results[c]["Xo"] for c in cores]

    nc5 = _get_prog("moe1", build_moe1_program)
    im = [dict(X=X[c], Wr=inputs["L1_ffn_gate_w"],
               gup=inputs["L1_ffn_gate_up_proj"],
               dn=inputs["L1_ffn_down_proj"]) for c in cores]
    r = run_bass_kernel_spmd(nc5, im, core_ids=cores)
    out = np.stack([r.results[c]["Xo"] for c in cores]).reshape(B, S, D)
    return out
